# version 10
# baseline (speedup 1.0000x reference)
import numpy as np
import ml_dtypes
import concourse.bass as bass
import concourse.mybir as mybir
from concourse.bass_utils import run_bass_kernel_spmd
from concourse import library_config

F32 = mybir.dt.float32
BF16 = mybir.dt.bfloat16
I32 = mybir.dt.int32
U32 = mybir.dt.uint32
I16 = mybir.dt.int16
AF = mybir.ActivationFunctionType
OP = mybir.AluOpType
AXX = mybir.AxisListType.X

B = 2
S = 2048
D = 1024
H = 16
DH = 64
HP = 8
E = 16
DFF = 2048
CAP = 256
RH = 128
NT = S // 128
NI = S // 512
DC = D // 128
FC = DFF // 128

SD = DH ** -0.5
LAM = 1.0507009873554804934193349852946
ALPHA = 1.6732632423543772848170429916717
CA = LAM * S ** -0.5
CB = LAM * ALPHA * S ** -0.5
EPS = 1e-5
NSPLIT = 2

_DBG = False
_PH = 4
_NCORES = 8
_ATT = 4
_SEVAC = 1
_SIMTP = 1
_NHP = 8


def build_nc():
    nc = bass.Bass("TRN2", target_bir_lowering=False)
    dk = "ExternalOutput" if _DBG else "Internal"

    x_h = nc.dram_tensor("x", [B, S, D], F32, kind="ExternalInput")
    wqk_h = nc.dram_tensor("wqkT", [128, DC, 2048], F32, kind="ExternalInput")
    bqk_h = nc.dram_tensor("bqk", [128, 16], F32, kind="ExternalInput")
    rw1_h = nc.dram_tensor("rw1T", [128, DC, RH], F32, kind="ExternalInput")
    rw2_h = nc.dram_tensor("rw2", [RH, E], F32, kind="ExternalInput")
    w1_h = nc.dram_tensor("w1h", [E, 128, DC, DFF], F32, kind="ExternalInput")
    w2_h = nc.dram_tensor("w2h", [E, 128, DC, DFF], F32, kind="ExternalInput")
    w3_h = nc.dram_tensor("w3h", [E, 128, FC, D], F32, kind="ExternalInput")
    idb_h = nc.dram_tensor("idbf", [128, 128], BF16, kind="ExternalInput")
    idf_h = nc.dram_tensor("idf32", [128, 128], F32, kind="ExternalInput")
    y_h = nc.dram_tensor("y", [B, S, D], F32, kind="ExternalOutput")
    x1d_h = nc.dram_tensor("x1d", [B, S, D], F32, kind=dk)
    ln2d_h = nc.dram_tensor("ln2d", [B, S, D], BF16, kind=dk)
    accd_h = nc.dram_tensor("accd", [B, S, D], F32, kind=dk)

    x_d, y_d = x_h.ap(), y_h.ap()
    x1_d, ln2_d, acc_d = x1d_h.ap(), ln2d_h.ap(), accd_h.ap()
    ln2f_d = ln2d_h.ap().rearrange("b s d -> (b s) d")
    accf_d = accd_h.ap().rearrange("b s d -> (b s) d")

    KB = 1024
    placements = []

    def place(name, shape, dtype, off):
        h = nc.virtual_sbuf_tensor(name, shape, dtype)
        placements.append((h, off))
        return h

    OA = 0
    wqk = place("wqk", [128, DC * 2048], BF16, OA)
    wb12_0 = place("wb12_0", [128, DC * DFF], BF16, OA)
    wb3_0 = place("wb3_0", [128, FC * D], BF16, OA)
    OB = 32 * KB
    k_fm = place("k_fm", [128, DC * 2048], BF16, OB)
    wb12_1 = place("wb12_1", [128, DC * DFF], BF16, OB)
    wb3_1 = place("wb3_1", [128, FC * D], BF16, OB)
    OC = 64 * KB
    q_fm = place("q_fm", [128, DC * 2048], BF16, OC)
    ei0 = place("ei0", [128, DC * 512], BF16, OC)
    ei1 = place("ei1", [128, DC * 512], BF16, OC + 8 * KB)
    h_fm = place("h_fm", [128, FC * 512], BF16, OC + 16 * KB)
    OD = 96 * KB
    ln1_tm = place("ln1_tm", [128, NT * D], BF16, OD)
    dv = [place(f"dv{i}", [128, D], F32, OD + i * 4 * KB) for i in range(4)]
    acc_t = [place(f"acct{i}", [128, D], F32, OD + 16 * KB + i * 4 * KB) for i in range(2)]
    x1f_t = [place(f"x1ft{i}", [128, D], F32, OD + 24 * KB + i * 4 * KB) for i in range(2)]
    OE = 128 * KB
    slab = place("slab", [128, DC * 512], BF16, OE)
    scrb = place("scrb", [128, D], BF16, OE)
    scrf = place("scrf", [128, D], F32, OE)
    ln_t = place("ln_t", [128, D], F32, OE + 4 * KB)
    sil_all = place("sil_all", [128, FC * 512], BF16, OE)
    out_t = [place("outt0", [128, D], F32, OE + 4 * KB),
             place("outt1", [128, D], F32, 136 * KB)]
    OF = 136 * KB
    out_fm = place("out_fm", [128, HP * 512], BF16, OF)
    ei_tm = place("ei_tm", [128, 4, D], BF16, 144 * KB)
    OG = 144 * KB
    eb = [place(f"eb{i}", [128, 2 * 512], BF16, OG + i * 2 * KB) for i in range(2)]
    apb = [place(f"apb{i}", [128, 2 * 512], BF16, OG + 4 * KB + i * 2 * KB) for i in range(2)]
    bpb = [place(f"bpb{i}", [128, 2 * 512], BF16, OG + 8 * KB + i * 2 * KB) for i in range(2)]
    OX = 156 * KB
    x_t = [place(f"xt{i}", [128, D], F32, OX + i * 4 * KB) for i in range(2)]
    OX1 = 164 * KB
    x1_t = [place(f"x1t{i}", [128, D], F32, OX1 + i * 4 * KB) for i in range(2)]
    OL = 172 * KB
    logits = place("logits", [64, S], F32, OL)
    zero_sb = place("zero_sb", [128, D], F32, OF)

    off = [180 * KB]

    def salloc(name, shape, dtype):
        per = int(np.prod(shape[1:])) * mybir.dt.size(dtype)
        o = (off[0] + 31) // 32 * 32
        h = place(name, shape, dtype, o)
        off[0] = o + per
        return h, o

    id_bf, _ = salloc("id_bf", [128, 128], BF16)
    id_f, _ = salloc("id_f", [128, 128], F32)
    bqk, _ = salloc("bqk_sb", [128, 16], F32)
    rw2, _ = salloc("rw2_sb", [RH, E], BF16)
    rw1, _ = salloc("rw1_sb", [128, DC * RH], BF16)
    stc, _ = salloc("stc", [128, 32], F32)
    kbar, _ = salloc("kbar", [128, DC], F32)
    colsv, _ = salloc("colsv", [128, DC], F32)
    corr, _ = salloc("corr", [128, DC], F32)
    tmpc, _ = salloc("tmpc", [128, 2], F32)
    r1bf, _ = salloc("r1bf", [128, 128], BF16)
    st2, _ = salloc("st2", [64, 8], F32)
    vals, OVALS = salloc("vals", [64, CAP], F32)
    idx, _ = salloc("idx", [64, CAP], U32)
    idxf, OIDXF = salloc("idxf", [64, CAP], F32)
    idxT_i, _ = salloc("idxT_i", [128, 128], U32)
    idxT_h, _ = salloc("idxT_h", [128, 128], I16)
    ewT, _ = salloc("ewT", [128, 128], F32)
    idxg, OIDXG = salloc("idxg", [128, 1024], I16)
    cst, _ = salloc("cst", [128, 8], F32)
    out_tm = place("out_tm", [128, D], BF16, OVALS)
    ln2_t = place("ln2_t", [128, D], BF16, OIDXF)
    ln2f_t = place("ln2f_t", [128, D], BF16, OIDXG)
    assert off[0] <= 192 * KB, off[0]

    arena_ctx = nc.sbuf_tensor("arena", [128, 192 * KB], mybir.dt.uint8)
    arena_h = arena_ctx.__enter__()
    base = nc.lookup_mloc(arena_h).addr
    for h, o in placements:
        nc._tensor(h.name, h.shape, h.dtype, type="SB")
        ml = nc.lookup_mloc(h)
        ml.allocated = True
        ml.addr = base + o

    ps_ctx = [nc.psum_tensor(f"ps{i}", [128, 512], F32) for i in range(6)]
    ps = [c.__enter__() for c in ps_ctx]
    ps6_ctx = nc.psum_tensor("ps6b", [128, 1024], BF16)
    psb6 = ps6_ctx.__enter__()
    ps_ctx.append(ps6_ctx)
    ps7_ctx = nc.psum_tensor("ps7", [128, 512], F32)
    ps7 = ps7_ctx.__enter__()
    ps_ctx.append(ps7_ctx)
    ps = ps + [None, ps7]

    sem_ctx = {}
    sems = {}
    for nme in ["qcst", "qx0", "qx1", "qzf", "qs0", "qs1", "qlns", "qidx", "qwp",
                "qw0", "qw1", "qw2", "qg0", "qg1", "qc0", "qc1", "qf0", "qf1",
                "t", "v", "s", "vv"]:
        c = nc.semaphore(f"sem_{nme}")
        sem_ctx[nme] = c
        sems[nme] = c.__enter__()
    CNT = {k: 0 for k in sems}

    block_ctx = nc.Block()
    block = block_ctx.__enter__()

    def inc(instr, nme, k=1):
        instr.then_inc(sems[nme], k)
        CNT[nme] += k
        return CNT[nme]

    def winc(instr, nme):
        return inc(instr, nme, 16)

    def W(eng, nme, val):
        if val > 0:
            eng.wait_ge(sems[nme], val)

    def wfree(eng, conds):
        if isinstance(conds, tuple):
            conds = [conds]
        for nme, v in conds:
            W(eng, nme, v)

    def onT(f):
        block.tensor(f)

    def onV(f):
        block.vector(f)

    def onS(f):
        block.scalar(f)

    def onQ(f):
        block.sync(f)

    def onG(f):
        block.gpsimd(f)

    def vnext(eng, instr):
        inc(instr, "vv")
        W(eng, "vv", CNT["vv"])

    def ln_stats_v(eng, src, sp, dump):
        eng.reduce_sum(stc[:, sp:sp + 1], src, axis=AXX)
        vnext(eng, eng.scalar_tensor_tensor(out=dump, in0=src, scalar=1.0, in1=src,
                                            op0=OP.mult, op1=OP.mult, accum_out=stc[:, sp + 1:sp + 2]))
        eng.tensor_scalar(stc[:, sp + 2:sp + 3], stc[:, sp:sp + 1], 1.0 / D, None, op0=OP.mult)
        vnext(eng, eng.tensor_scalar(stc[:, sp + 3:sp + 4], stc[:, sp + 1:sp + 2], 1.0 / D, None, op0=OP.mult))
        vnext(eng, eng.tensor_tensor(stc[:, sp + 4:sp + 5], stc[:, sp + 2:sp + 3], stc[:, sp + 2:sp + 3], op=OP.mult))
        return eng.tensor_sub(stc[:, sp + 5:sp + 6], stc[:, sp + 3:sp + 4], stc[:, sp + 4:sp + 5])

    def ln_rstd_v(eng, sp):
        vnext(eng, eng.reciprocal(stc[:, sp + 7:sp + 8], stc[:, sp + 6:sp + 7]))
        return eng.scalar_tensor_tensor(out=stc[:, sp + 8:sp + 9], in0=stc[:, sp + 2:sp + 3],
                                        scalar=-1.0, in1=stc[:, sp + 7:sp + 8],
                                        op0=OP.mult, op1=OP.mult)

    def prelude_q(eng):
        winc(eng.dma_start(out=id_bf[:, :], in_=idb_h.ap()[:, :]), "qcst")
        winc(eng.dma_start(out=id_f[:, :], in_=idf_h.ap()[:, :]), "qcst")
        winc(eng.dma_start(out=bqk[:, :], in_=bqk_h.ap()[:, :]), "qcst")
    onQ(prelude_q)
    CONSTS_Q = CNT["qcst"]

    def prelude_g(eng):
        winc(eng.dma_start(out=wqk[:, :], in_=wqk_h.ap()[:, :, :]), "qwp")
        winc(eng.dma_start(out=rw1[:, :], in_=rw1_h.ap()[:, :, :]), "qwp")
        winc(eng.dma_start(out=rw2[:, :], in_=rw2_h.ap()[:, :]), "qwp")
    onG(prelude_g)
    WQK_LOADED = CNT["qwp"]

    CONSTS = [EPS, CA]
    for ci, cv in enumerate(CONSTS):
        nc.const_aps.register(F32, cv, cst[:, ci:ci + 1])

    def prelude_v(eng):
        eng.memset(zero_sb[:, :], 0.0)
        eng.memset(logits[:, :], 0.0)
        for ci, cv in enumerate(CONSTS):
            eng.memset(cst[:, ci:ci + 1], cv)
        vnext(eng, eng.memset(colsv[:, :], 0.0))
        inc(eng.memset(tmpc[:, 1:2], 0.0), "v")
    onV(prelude_v)
    ZMEM = CNT["v"]

    def zfill(eng):
        W(eng, "v", ZMEM)
        for b in range(B):
            for t in range(NT):
                winc(eng.dma_start(out=acc_d[b, t * 128:(t + 1) * 128, :], in_=zero_sb[:, :]), "qzf")
    onQ(zfill)
    ZFILL_DONE = CNT["qzf"]

    xfree = [[], []]
    x1free = [[], []]
    ln2free = []
    out_tm_free = []
    ln2f_free = []
    outtfree = [[], []]
    psfree = [[("v", 0)]] * 8
    psfree = [[] for _ in range(8)]
    ebfree = [[], []]
    apfree = [[], []]
    LNST = {}
    lnst = [0, 0]
    x1st = {}
    G = {"proj_i": 0, "a6T": 0, "a6V": 0, "slabsum": 0}

    def attention(b):
        nonlocal ln2free, out_tm_free, ln2f_free
        for i in range(NI):
            appl = [0] * 4
            for tc in range(4):
                t = i * 4 + tc
                par = t % 2
                sp = par * 16

                def ld(eng, b=b, t=t, par=par, fr=list(xfree[par])):
                    wfree(eng, fr)
                    winc(eng.dma_start(out=x_t[par][:, :], in_=x_d[b, t * 128:(t + 1) * 128, :]), f"qx{par}")
                onQ(ld)
                xq = CNT[f"qx{par}"]

                def stats(eng, par=par, sp=sp, xq=xq, fr=list(x1free[0]), b=b, t=t):
                    W(eng, f"qx{par}", xq)
                    if b == 0 and t == 0:
                        W(eng, "qcst", CONSTS_Q)
                    wfree(eng, fr)
                    inc(ln_stats_v(eng, x_t[par][:, :], sp, x1_t[0][:, :]), "v")
                onV(stats)
                vst = CNT["v"]

                def rstd(eng, sp=sp, vst=vst):
                    W(eng, "v", vst)
                    inc(eng.activation(out=stc[:, sp + 6:sp + 7], in_=stc[:, sp + 5:sp + 6],
                                       func=AF.Sqrt, bias=EPS), "s")
                onS(rstd)
                sst = CNT["s"]

                def rcpb(eng, sp=sp, sst=sst):
                    W(eng, "s", sst)
                    inc(ln_rstd_v(eng, sp), "v")
                onV(rcpb)
                vcb = CNT["v"]

                def apply(eng, par=par, sp=sp, t=t, vcb=vcb):
                    W(eng, "v", vcb)
                    inc(eng.activation(out=ln1_tm[:, t * D:(t + 1) * D], in_=x_t[par][:, :], func=AF.Identity,
                                       bias=stc[:, sp + 8:sp + 9], scale=stc[:, sp + 7:sp + 8]), "s")
                onS(apply)
                appl[tc] = CNT["s"]
                xfree[par] = [("s", CNT["s"])]

            for tc in range(4):
                t = i * 4 + tc
                for dq in range(2):
                    hb = dq % 2

                    def tpose(eng, t=t, dq=dq, hb=hb, ap4=appl[tc], fr=list(psfree[6 + hb])):
                        W(eng, "s", ap4)
                        wfree(eng, fr)
                        for c in range(4):
                            dc = dq * 4 + c
                            ins = eng.transpose(psb6[:, hb * 512 + c * 128: hb * 512 + (c + 1) * 128],
                                                ln1_tm[:, t * D + dc * 128: t * D + dc * 128 + 128],
                                                id_bf[:, :])
                        inc(ins, "t")
                    onT(tpose)
                    tdone = CNT["t"]

                    def tevac(eng, tc=tc, dq=dq, hb=hb, tdone=tdone,
                              pj=G["proj_i"], ss=G["slabsum"], av=G["a6V"]):
                        W(eng, "t", tdone)
                        W(eng, "t", pj)
                        W(eng, "v", ss)
                        W(eng, "v", av)
                        for c in range(4):
                            dc = dq * 4 + c
                            ins = eng.activation(out=slab[:, dc * 512 + tc * 128: dc * 512 + tc * 128 + 128],
                                                 in_=psb6[:, hb * 512 + c * 128: hb * 512 + (c + 1) * 128], func=AF.Copy)
                        inc(ins, "s")
                    onS(tevac)
                    psfree[6 + hb] = [("s", CNT["s"])]
            slab_ready = CNT["s"]

            def slabsum(eng, sr=slab_ready):
                W(eng, "s", sr)
                for dc in range(DC):
                    vnext(eng, eng.reduce_sum(tmpc[:, 0:1], slab[:, dc * 512:(dc + 1) * 512], axis=AXX))
                    ins = eng.tensor_add(colsv[:, dc:dc + 1], colsv[:, dc:dc + 1], tmpc[:, 0:1])
                    if dc < DC - 1:
                        vnext(eng, ins)
                inc(ins, "v")
            onV(slabsum)
            G["slabsum"] = CNT["v"]

            for fc in (range(16) if _ATT >= 2 else []):
                bank = (i * 16 + fc) % 2

                def proj(eng, fc=fc, bank=bank, sr=slab_ready, i=i, fr=list(psfree[bank])):
                    W(eng, "s", sr)
                    if i == 0 and fc == 0:
                        W(eng, "qwp", WQK_LOADED)
                    wfree(eng, fr)
                    for dc in range(DC):
                        ins = eng.matmul(ps[bank][:, :],
                                         lhsT=wqk[:, dc * 2048 + fc * 128: dc * 2048 + fc * 128 + 128],
                                         rhs=slab[:, dc * 512:(dc + 1) * 512],
                                         start=(dc == 0), stop=(dc == DC - 1))
                    inc(ins, "t")
                onT(proj)
                pj = CNT["t"]

                def pevac(eng, fc=fc, bank=bank, i=i, pj=pj):
                    W(eng, "t", pj)
                    dst = q_fm if fc < 8 else k_fm
                    c = fc if fc < 8 else fc - 8
                    inc(eng.activation(out=dst[:, c * 2048 + i * 512: c * 2048 + i * 512 + 512],
                                       in_=ps[bank][:, :], func=AF.Identity, bias=bqk[:, fc:fc + 1]), "s")
                onS(pevac)
                psfree[bank] = [("s", CNT["s"])]
            G["proj_i"] = CNT["t"]
        PROJ_EVAC = CNT["s"]

        def kcenter(eng, pe=PROJ_EVAC):
            W(eng, "s", pe)
            W(eng, "vv", CNT["vv"])
            for dc in range(DC):
                vnext(eng, eng.reduce_sum(kbar[:, dc:dc + 1], k_fm[:, dc * 2048:(dc + 1) * 2048], axis=AXX))
                vnext(eng, eng.tensor_scalar(kbar[:, dc:dc + 1], kbar[:, dc:dc + 1], 1.0 / S, None, op0=OP.mult))
                eng.tensor_scalar(k_fm[:, dc * 2048:(dc + 1) * 2048], k_fm[:, dc * 2048:(dc + 1) * 2048],
                                  kbar[:, dc:dc + 1], SD, op0=OP.subtract, op1=OP.mult)
            vnext(eng, eng.tensor_scalar(corr[:, :], colsv[:, :], CB, None, op0=OP.mult))
            inc(eng.memset(colsv[:, :], 0.0), "v")
        if _ATT >= 2:
            onV(kcenter)
        KC = CNT["v"]

        g = 0
        simdone = {}
        expdone = {}
        aprdone = {}
        av_last = 0
        for i in range(NI):
            for hp in range(HP if _ATT >= 4 else _NHP):
                for j in (range(NT + 1) if _ATT >= 3 else []):
                    if j < NT:
                        pA, pB = g % 2, 2 + g % 2

                        def sim(eng, i=i, hp=hp, j=j, g=g, pA=pA, pB=pB, kc=KC,
                                frA=list(psfree[pA]), frB=list(psfree[pB])):
                            if g == 0:
                                W(eng, "v", kc)
                            if g >= 2:
                                W(eng, "s", expdone[g - 2])
                                W(eng, "v", aprdone[g - 2])
                            else:
                                wfree(eng, frA)
                                wfree(eng, frB)
                            if _SIMTP:
                                eng.matmul(ps[pA][:, :],
                                           lhsT=k_fm[0:64, hp * 2048 + j * 128: hp * 2048 + j * 128 + 128],
                                           rhs=q_fm[0:64, hp * 2048 + i * 512: hp * 2048 + i * 512 + 512],
                                           start=True, stop=True, tile_position=(0, 0))
                                inc(eng.matmul(ps[pB][:, :],
                                               lhsT=k_fm[64:128, hp * 2048 + j * 128: hp * 2048 + j * 128 + 128],
                                               rhs=q_fm[64:128, hp * 2048 + i * 512: hp * 2048 + i * 512 + 512],
                                               start=True, stop=True, tile_position=(64, 0)), "t")
                            else:
                                eng.matmul(ps[pA][:, :],
                                           lhsT=k_fm[:, hp * 2048 + j * 128: hp * 2048 + j * 128 + 128],
                                           rhs=q_fm[:, hp * 2048 + i * 512: hp * 2048 + i * 512 + 512],
                                           start=True, stop=True)
                                inc(eng.matmul(ps[pB][:, :],
                                               lhsT=k_fm[:, hp * 2048 + j * 128: hp * 2048 + j * 128 + 128],
                                               rhs=q_fm[:, hp * 2048 + i * 512: hp * 2048 + i * 512 + 512],
                                               start=True, stop=True), "t")
                        onT(sim)
                        simdone[g] = CNT["t"]
                        s_does_a = (j % 8) < NSPLIT
                        ep = g % 2

                        def sevac(eng, g=g, pA=pA, pB=pB, ep=ep, s_does_a=s_does_a,
                                  fre=list(ebfree[ep]), fra=list(apfree[ep])):
                            W(eng, "t", simdone[g])
                            wfree(eng, fre)
                            fE = AF.Exp if _SEVAC else AF.Copy
                            eng.activation(out=eb[ep][:, 0:512], in_=ps[pA][:, :], func=fE)
                            ins = eng.activation(out=eb[ep][:, 512:1024], in_=ps[pB][:, :], func=fE)
                            if s_does_a:
                                wfree(eng, fra)
                                if _SEVAC:
                                    eng.activation(out=apb[ep][:, 0:512], in_=ps[pA][:, :], func=AF.Relu, scale=CA)
                                    ins = eng.activation(out=apb[ep][:, 512:1024], in_=ps[pB][:, :], func=AF.Relu, scale=CA)
                                else:
                                    eng.activation(out=apb[ep][:, 0:512], in_=ps[pA][:, :], func=AF.Copy)
                                    ins = eng.activation(out=apb[ep][:, 512:1024], in_=ps[pB][:, :], func=AF.Copy)
                            inc(ins, "s")
                        onS(sevac)
                        expdone[g] = CNT["s"]

                        def vevac(eng, g=g, pA=pA, pB=pB, ep=ep, s_does_a=s_does_a,
                                  fra=list(apfree[ep])):
                            if not s_does_a:
                                W(eng, "t", simdone[g])
                                W(eng, "s", expdone[g])
                                wfree(eng, fra)
                                if _SEVAC:
                                    eng.tensor_scalar(apb[ep][:, 0:512], ps[pA][:, :], 0.0, CA, op0=OP.max, op1=OP.mult)
                                    eng.tensor_scalar(apb[ep][:, 512:1024], ps[pB][:, :], 0.0, CA, op0=OP.max, op1=OP.mult)
                                else:
                                    eng.tensor_copy(apb[ep][:, 0:512], ps[pA][:, :])
                                    eng.tensor_copy(apb[ep][:, 512:1024], ps[pB][:, :])
                            W(eng, "s", expdone[g])
                            if _SEVAC:
                                eng.tensor_scalar(bpb[ep][:, 0:512], eb[ep][:, 0:512], 1.0, CB, op0=OP.min, op1=OP.mult)
                                inc(eng.tensor_scalar(bpb[ep][:, 512:1024], eb[ep][:, 512:1024], 1.0, CB,
                                                      op0=OP.min, op1=OP.mult), "v")
                            else:
                                eng.tensor_copy(bpb[ep][:, 0:512], eb[ep][:, 0:512])
                                inc(eng.tensor_copy(bpb[ep][:, 512:1024], eb[ep][:, 512:1024]), "v")
                        onV(vevac)
                        aprdone[g] = CNT["v"]
                        ebfree[ep] = [("v", CNT["v"])]
                        psfree[pA] = [("s", expdone[g]), ("v", aprdone[g])]
                        psfree[pB] = [("s", expdone[g]), ("v", aprdone[g])]
                    if j > 0 and _ATT >= 4:
                        ja = j - 1
                        ga = g - 1
                        ep = ga % 2

                        def attnv(eng, hp=hp, ja=ja, ga=ga, ep=ep,
                                  fr4=list(psfree[4]), fr5=list(psfree[5])):
                            W(eng, "v", aprdone[ga])
                            W(eng, "s", expdone[ga])
                            if ja == 0:
                                wfree(eng, fr4)
                                wfree(eng, fr5)
                            vA = ln1_tm[:, ja * D + hp * 128: ja * D + hp * 128 + 64]
                            vB = ln1_tm[:, ja * D + hp * 128 + 64: ja * D + hp * 128 + 128]
                            st = (ja == 0)
                            sp_ = (ja == NT - 1)
                            eng.matmul(ps[4][0:64, :], lhsT=vA, rhs=apb[ep][:, 0:512],
                                       start=st, stop=False, tile_position=(0, 0))
                            eng.matmul(ps[5][64:128, :], lhsT=vB, rhs=apb[ep][:, 512:1024],
                                       start=st, stop=False, tile_position=(0, 64))
                            eng.matmul(ps[4][0:64, :], lhsT=vA, rhs=bpb[ep][:, 0:512],
                                       start=False, stop=sp_, tile_position=(0, 0))
                            inc(eng.matmul(ps[5][64:128, :], lhsT=vB, rhs=bpb[ep][:, 512:1024],
                                           start=False, stop=sp_, tile_position=(0, 64)), "t")
                        onT(attnv)
                        av_last = CNT["t"]
                        apfree[ep] = [("t", CNT["t"])]
                    if j < NT:
                        g += 1

                def oevac(eng, hp=hp, i=i, av=av_last, a6T=G["a6T"], b=b):
                    W(eng, "t", av)
                    W(eng, "t", a6T)
                    if b == 0 and i == 0:
                        W(eng, "qzf", ZFILL_DONE)
                    eng.tensor_scalar(out_fm[0:64, hp * 512:(hp + 1) * 512], ps[4][0:64, :],
                                      corr[0:64, hp:hp + 1], None, op0=OP.subtract)
                    inc(eng.tensor_scalar(out_fm[64:128, hp * 512:(hp + 1) * 512], ps[5][64:128, :],
                                          corr[64:128, hp:hp + 1], None, op0=OP.subtract), "v")
                if _ATT >= 4:
                    onV(oevac)
                    psfree[4] = [("v", CNT["v"])]
                    psfree[5] = [("v", CNT["v"])]
            OUTE = CNT["v"]

            for tc in range(4):
                t = i * 4 + tc
                par = t % 2
                sp = par * 16

                def ldx(eng, b=b, t=t, par=par, fr=list(xfree[par])):
                    wfree(eng, fr)
                    winc(eng.dma_start(out=x_t[par][:, :], in_=x_d[b, t * 128:(t + 1) * 128, :]), f"qx{par}")
                onQ(ldx)
                xq = CNT[f"qx{par}"]

                ev = 0
                for dq in range(2):
                    hb = dq % 2

                    def a6t(eng, tc=tc, dq=dq, hb=hb, oe=OUTE, fr=list(psfree[6 + hb])):
                        W(eng, "v", oe)
                        wfree(eng, fr)
                        for c in range(4):
                            dc = dq * 4 + c
                            ins = eng.transpose(psb6[:, hb * 512 + c * 128: hb * 512 + (c + 1) * 128],
                                                out_fm[:, dc * 512 + tc * 128: dc * 512 + tc * 128 + 128],
                                                id_bf[:, :])
                        inc(ins, "t")
                    onT(a6t)
                    td = CNT["t"]

                    def a6e(eng, dq=dq, hb=hb, td=td, fr=list(out_tm_free)):
                        W(eng, "t", td)
                        if dq == 0:
                            wfree(eng, fr)
                        inc(eng.activation(out=out_tm[:, dq * 512:(dq + 1) * 512],
                                           in_=psb6[:, hb * 512: hb * 512 + 512], func=AF.Copy), "s")
                    onS(a6e)
                    psfree[6 + hb] = [("s", CNT["s"])]
                    ev = CNT["s"]

                def a6v(eng, sp=sp, ev=ev):
                    W(eng, "s", ev)
                    inc(ln_stats_v(eng, out_tm[:, :], sp, scrb[:, :]), "v")
                onV(a6v)
                out_tm_free = [("v", CNT["v"])]
                vst = CNT["v"]

                def a6s(eng, sp=sp, vst=vst):
                    W(eng, "v", vst)
                    inc(eng.activation(out=stc[:, sp + 6:sp + 7], in_=stc[:, sp + 5:sp + 6],
                                       func=AF.Sqrt, bias=EPS), "s")
                onS(a6s)
                sst = CNT["s"]

                def a6r(eng, sp=sp, sst=sst):
                    W(eng, "s", sst)
                    inc(ln_rstd_v(eng, sp), "v")
                onV(a6r)
                vr = CNT["v"]

                def a6ap(eng, sp=sp, vr=vr):
                    W(eng, "v", vr)
                    inc(eng.activation(out=ln_t[:, :], in_=out_tm[:, :], func=AF.Identity,
                                       bias=stc[:, sp + 8:sp + 9], scale=stc[:, sp + 7:sp + 8]), "s")
                onS(a6ap)
                out_tm_free.append(("s", CNT["s"]))
                sap = CNT["s"]

                def a6add(eng, par=par, sp=sp, sap=sap, xq=xq, fr=list(x1free[par])):
                    W(eng, "s", sap)
                    W(eng, f"qx{par}", xq)
                    wfree(eng, fr)
                    vnext(eng, eng.tensor_add(x1_t[par][:, :], ln_t[:, :], x_t[par][:, :]))
                    inc(ln_stats_v(eng, x1_t[par][:, :], sp, scrf[:, :]), "v")
                onV(a6add)
                xfree[par] = [("v", CNT["v"])]
                vadd = CNT["v"]

                def x1store(eng, b=b, t=t, par=par, vadd=vadd):
                    W(eng, "v", vadd)
                    winc(eng.dma_start(out=x1_d[b, t * 128:(t + 1) * 128, :], in_=x1_t[par][:, :]), f"qs{par}")
                onQ(x1store)
                x1st[(b, t)] = (f"qs{par}", CNT[f"qs{par}"])

                def l2s(eng, sp=sp, vadd=vadd):
                    W(eng, "v", vadd)
                    inc(eng.activation(out=stc[:, sp + 6:sp + 7], in_=stc[:, sp + 5:sp + 6],
                                       func=AF.Sqrt, bias=EPS), "s")
                onS(l2s)
                sst2 = CNT["s"]

                def l2r(eng, sp=sp, sst2=sst2):
                    W(eng, "s", sst2)
                    inc(ln_rstd_v(eng, sp), "v")
                onV(l2r)
                vr2 = CNT["v"]

                def l2ap(eng, par=par, sp=sp, vr2=vr2, fr=list(ln2free)):
                    W(eng, "v", vr2)
                    wfree(eng, fr)
                    inc(eng.activation(out=ln2_t[:, :], in_=x1_t[par][:, :], func=AF.Identity,
                                       bias=stc[:, sp + 8:sp + 9], scale=stc[:, sp + 7:sp + 8]), "s")
                onS(l2ap)
                x1free[par] = [("s", CNT["s"]), x1st[(b, t)]]
                sl2 = CNT["s"]

                def l2store(eng, b=b, t=t, sl2=sl2):
                    W(eng, "s", sl2)
                    winc(eng.dma_start(out=ln2_d[b, t * 128:(t + 1) * 128, :], in_=ln2_t[:, :]), "qlns")
                onQ(l2store)
                lnst[b] = CNT["qlns"]
                ln2free = [("qlns", CNT["qlns"])]

                for dq in range(2):
                    hb = dq % 2

                    def l2t(eng, dq=dq, hb=hb, sl2=sl2, fr=list(psfree[6 + hb])):
                        W(eng, "s", sl2)
                        wfree(eng, fr)
                        for c in range(4):
                            dc = dq * 4 + c
                            ins = eng.transpose(psb6[:, hb * 512 + c * 128: hb * 512 + (c + 1) * 128],
                                                ln2_t[:, dc * 128:(dc + 1) * 128], id_bf[:, :])
                        inc(ins, "t")
                    onT(l2t)
                    td = CNT["t"]

                    def l2te(eng, dq=dq, hb=hb, td=td, fr=list(ln2f_free)):
                        W(eng, "t", td)
                        if dq == 0:
                            wfree(eng, fr)
                        for c in range(4):
                            dc = dq * 4 + c
                            ins = eng.activation(out=ln2f_t[:, dc * 128:(dc + 1) * 128],
                                                 in_=psb6[:, hb * 512 + c * 128: hb * 512 + (c + 1) * 128], func=AF.Copy)
                        inc(ins, "s")
                    onS(l2te)
                    psfree[6 + hb] = [("s", CNT["s"])]
                ln2free.append(("t", 0))
                L2F = CNT["s"]

                prevq = (b, t - 4) if t >= 4 else ((0, t + 12) if b == 1 else None)

                def router(eng, L2F=L2F, t=t, prevq=prevq):
                    W(eng, "s", L2F)
                    if prevq is not None:
                        W(eng, "v", LNST[prevq])
                    tq = t % 4
                    for dc in range(DC):
                        ins = eng.matmul(ps7[:, tq * 128:(tq + 1) * 128], lhsT=rw1[:, dc * RH:(dc + 1) * RH],
                                         rhs=ln2f_t[:, dc * 128:(dc + 1) * 128],
                                         start=(dc == 0), stop=(dc == DC - 1))
                    inc(ins, "t")
                onT(router)
                rt = CNT["t"]
                ln2free[-1] = ("t", rt)
                ln2f_free = [("t", rt)]

                def rsilu(eng, rt=rt, t=t):
                    W(eng, "t", rt)
                    inc(eng.activation(out=r1bf[:, :], in_=ps7[:, (t % 4) * 128:(t % 4) * 128 + 128], func=AF.Silu), "s")
                onS(rsilu)
                rs = CNT["s"]

                def rlog(eng, rs=rs, b=b, t=t):
                    W(eng, "s", rs)
                    tp = (0, 0) if b == 0 else (0, 32)
                    r0 = 0 if b == 0 else 32
                    inc(eng.matmul(ps7[r0:r0 + 16, (t % 4) * 128:(t % 4) * 128 + 128],
                                   lhsT=rw2[:, :], rhs=r1bf[:, :], start=True, stop=True,
                                   tile_position=tp), "t")
                onT(rlog)
                lg = CNT["t"]

                def rlev(eng, lg=lg, b=b, t=t):
                    W(eng, "t", lg)
                    r0 = 0 if b == 0 else 32
                    inc(eng.tensor_copy(out=logits[b * 32:b * 32 + 16, t * 128:(t + 1) * 128],
                                        in_=ps7[r0:r0 + 16, (t % 4) * 128:(t % 4) * 128 + 128]), "v")
                onV(rlev)
                LNST[(b, t)] = CNT["v"]
            G["a6T"] = CNT["t"]
            G["a6V"] = CNT["v"]
        return CNT["t"]

    attention(0)
    PROJ_B1_MARK = None
    attention(1)
    PROJ_B1 = G["proj_i"]
    SIM_B1 = CNT["t"]
    ATT_S = CNT["s"]
    ATT_V = CNT["v"]

    tkv = CNT["v"]

    def topk1(eng):
        W(eng, "v", ATT_V)
        vnext(eng, eng.reduce_max(st2[:, 0:1], logits[:, :], axis=AXX))
        inc(eng.tensor_scalar(st2[:, 1:2], st2[:, 0:1], -1.0, None, op0=OP.mult), "v")
    if _PH >= 2:
        onV(topk1)
    tk1 = CNT["v"]

    def texp(eng, tk1=tk1):
        W(eng, "v", tk1)
        inc(eng.activation(out=logits[:, :], in_=logits[:, :], func=AF.Exp, bias=st2[:, 1:2]), "s")
    if _PH >= 2:
        onS(texp)
    tke = CNT["s"]

    def trounds(eng, tke=tke, l1=lnst[1]):
        W(eng, "s", tke)
        W(eng, "qlns", l1)
        eng.memset(idxg[:, :], 0)
        vnext(eng, eng.reduce_sum(st2[:, 2:3], logits[:, :], axis=AXX))
        vnext(eng, eng.reciprocal(st2[:, 3:4], st2[:, 2:3]))
        for r in range(32):
            vnext(eng, eng.max(out=vals[:, r * 8:(r + 1) * 8], in_=logits[:, :]))
            vnext(eng, eng.max_index(out=idx[:, r * 8:(r + 1) * 8], in_max=vals[:, r * 8:(r + 1) * 8],
                                     in_values=logits[:, :]))
            vnext(eng, eng.match_replace(out=logits[:, :], in_to_replace=vals[:, r * 8:(r + 1) * 8],
                                         in_values=logits[:, :], imm_value=-1.0))
        vnext(eng, eng.tensor_scalar(vals[:, :], vals[:, :], st2[:, 3:4], None, op0=OP.mult))
        vnext(eng, eng.reduce_sum(st2[:, 4:5], vals[:, :], axis=AXX))
        vnext(eng, eng.tensor_scalar(st2[:, 4:5], st2[:, 4:5], 1e-9, None, op0=OP.add))
        vnext(eng, eng.reciprocal(st2[:, 5:6], st2[:, 4:5]))
        vnext(eng, eng.tensor_scalar(vals[:, :], vals[:, :], st2[:, 5:6], None, op0=OP.mult))
        vnext(eng, eng.tensor_copy(out=idxf[:, :], in_=idx[:, :]))
        inc(eng.tensor_scalar(idxf[32:64, :], idxf[32:64, :], 2048.0, None, op0=OP.add), "v")
    if _PH >= 2:
        onV(trounds)
    tkr = CNT["v"]

    def ttrans(eng, tkr=tkr):
        W(eng, "v", tkr)
        eng.transpose(ps7[:, 0:64], idxf[0:64, 0:128], id_f[0:64, 0:64])
        eng.transpose(ps7[:, 64:128], idxf[0:64, 128:256], id_f[0:64, 0:64])
        eng.transpose(ps7[:, 128:192], vals[0:64, 0:128], id_f[0:64, 0:64])
        inc(eng.transpose(ps7[:, 192:256], vals[0:64, 128:256], id_f[0:64, 0:64]), "t")
    if _PH >= 2:
        onT(ttrans)
    tkt = CNT["t"]

    def tevac2(eng, tkt=tkt):
        W(eng, "t", tkt)
        eng.tensor_copy(out=idxT_i[:, 0:64], in_=ps7[:, 0:64])
        eng.tensor_copy(out=idxT_i[:, 64:128], in_=ps7[:, 64:128])
        eng.tensor_copy(out=ewT[:, 0:64], in_=ps7[:, 128:192])
        inc(eng.tensor_copy(out=ewT[:, 64:128], in_=ps7[:, 192:256]), "v")
    if _PH >= 2:
        onV(tevac2)
        psfree[6] = [("v", CNT["v"])]
    tkv = CNT["v"]

    IDXG_DONE = 0

    wb12 = [wb12_0, wb12_1]
    wb3 = [wb3_0, wb3_1]
    consumer_done = {}
    wload_cnt = {}

    def emit_wload(L):
        e, m = divmod(L, 3)
        buf = L % 2
        src = [w1_h, w2_h, w3_h][m]

        def wl(eng, e=e, m=m, buf=buf, L=L):
            if L == 0:
                W(eng, "t", PROJ_B1)
            elif L == 1:
                W(eng, "t", SIM_B1)
            else:
                W(eng, "t", consumer_done[L - 2])
            dst = wb12[buf] if m < 2 else wb3[buf]
            winc(eng.dma_start(out=dst[:, :], in_=src.ap()[e, :, :, :]), f"qw{L % 3}")
        onG(wl)
        wload_cnt[L] = (f"qw{L % 3}", CNT[f"qw{L % 3}"])

    eib = [ei0, ei1]
    gath_cnt = {}
    eiT_done = {}
    eiT_evac = {}
    eifm_free = {}

    def emit_gather(e):
        def gth(eng, e=e, prev=eiT_done.get(e - 1, 0)):
            if e == 0:
                W(eng, "t", SIM_B1)
                W(eng, "qlns", lnst[1])
                W(eng, "v", tkv)
            else:
                W(eng, "t", prev)
            for mt in range(4):
                bb, hh = divmod(mt, 2)
                be = bb * 32 + e
                winc(eng.indirect_dma_start(
                    out=ei_tm[:, mt, :],
                    out_offset=None,
                    in_=ln2f_d[:, :],
                    in_offset=bass.IndirectOffsetOnAxis(idxT_i[:, hh * 64 + be: hh * 64 + be + 1], 0)),
                    f"qg{e % 2}")
        onG(gth)
        gath_cnt[e] = (f"qg{e % 2}", CNT[f"qg{e % 2}"])

    def emit_eitrans(e):
        gq = gath_cnt[e]
        for mt in range(4):
            for dq in range(2):
                hb = dq % 2

                def eit(eng, e=e, mt=mt, dq=dq, hb=hb, gq=gq, fr=list(psfree[6 + hb])):
                    wfree(eng, [gq])
                    wfree(eng, fr)
                    if mt == 0 and dq == 0:
                        wfree(eng, eifm_free.get(e, []))
                    for c in range(4):
                        dc = dq * 4 + c
                        ins = eng.transpose(psb6[:, hb * 512 + c * 128: hb * 512 + (c + 1) * 128],
                                            ei_tm[:, mt, dc * 128:(dc + 1) * 128], id_bf[:, :])
                    inc(ins, "t")
                onT(eit)
                td = CNT["t"]

                def eie(eng, e=e, mt=mt, dq=dq, hb=hb, td=td):
                    W(eng, "t", td)
                    for c in range(4):
                        dc = dq * 4 + c
                        ins = eng.activation(out=eib[e % 2][:, dc * 512 + mt * 128: dc * 512 + mt * 128 + 128],
                                             in_=psb6[:, hb * 512 + c * 128: hb * 512 + (c + 1) * 128],
                                             func=AF.Copy)
                    inc(ins, "s")
                onS(eie)
                psfree[6 + hb] = [("s", CNT["s"])]
        eiT_done[e] = CNT["t"]
        eiT_evac[e] = CNT["s"]

    if _PH >= 3:
        emit_wload(0)
        emit_wload(1)
        emit_gather(0)
        emit_eitrans(0)
        emit_gather(1)
        emit_eitrans(1)

    sct_chain = {0: 0, 1: 0}
    dv_free = [[] for _ in range(4)]
    silfree = []
    down_prev = 0

    for e in (range(E) if _PH >= 3 else []):
        b12 = wb12[(3 * e) % 2]
        b22 = wb12[(3 * e + 1) % 2]
        b33 = wb3[(3 * e + 2) % 2]
        gateFc = {}
        valFc = {}
        silFc = {}

        for fc in range(FC):
            pg = fc % 2

            def gate(eng, e=e, fc=fc, pg=pg, b12=b12, fr=list(psfree[pg])):
                if fc == 0:
                    wfree(eng, [wload_cnt[3 * e]])
                    W(eng, "s", eiT_evac[e])
                    if e == 0:
                        W(eng, "s", ATT_S)
                        W(eng, "v", ATT_V)
                wfree(eng, fr)
                for dc in range(DC):
                    ins = eng.matmul(ps[pg][:, :],
                                     lhsT=b12[:, dc * DFF + fc * 128: dc * DFF + fc * 128 + 128],
                                     rhs=eib[e % 2][:, dc * 512:(dc + 1) * 512],
                                     start=(dc == 0), stop=(dc == DC - 1))
                inc(ins, "t")
            onT(gate)
            gateFc[fc] = CNT["t"]

            def gsilu(eng, fc=fc, pg=pg, fr=list(silfree)):
                W(eng, "t", gateFc[fc])
                wfree(eng, fr)
                inc(eng.activation(out=sil_all[:, fc * 512:(fc + 1) * 512], in_=ps[pg][:, :], func=AF.Silu), "s")
            onS(gsilu)
            silFc[fc] = CNT["s"]
            psfree[pg] = [("s", CNT["s"])]
        consumer_done[3 * e] = gateFc[FC - 1]
        if 3 * e + 2 < 3 * E:
            emit_wload(3 * e + 2)

        for fc in range(FC):
            pv = 2 + fc % 2

            def value(eng, e=e, fc=fc, pv=pv, b22=b22, fr=list(psfree[pv])):
                if fc == 0:
                    wfree(eng, [wload_cnt[3 * e + 1]])
                wfree(eng, fr)
                for dc in range(DC):
                    ins = eng.matmul(ps[pv][:, :],
                                     lhsT=b22[:, dc * DFF + fc * 128: dc * DFF + fc * 128 + 128],
                                     rhs=eib[e % 2][:, dc * 512:(dc + 1) * 512],
                                     start=(dc == 0), stop=(dc == DC - 1))
                inc(ins, "t")
            onT(value)
            valFc[fc] = CNT["t"]

            def hmul(eng, e=e, fc=fc, pv=pv, dp=down_prev):
                W(eng, "t", valFc[fc])
                W(eng, "s", silFc[fc])
                if fc == 0 and e > 0:
                    W(eng, "t", dp)
                inc(eng.tensor_tensor(out=h_fm[:, fc * 512:(fc + 1) * 512],
                                      in0=ps[pv][:, :], in1=sil_all[:, fc * 512:(fc + 1) * 512], op=OP.mult), "v")
            onV(hmul)
            psfree[pv] = [("v", CNT["v"])]
        consumer_done[3 * e + 1] = valFc[FC - 1]
        HD = CNT["v"]
        silfree = [("v", HD)]
        eifm_free[e + 2] = [("t", valFc[FC - 1])]

        if 3 * e + 3 < 3 * E:
            emit_wload(3 * e + 3)
        if e + 2 < E:
            emit_gather(e + 2)
            emit_eitrans(e + 2)

        downT = {}
        dvdone = {}
        for mt in range(4):
            bb, hh = divmod(mt, 2)
            for dhf in range(2):
                pd = 4 + (mt * 2 + dhf) % 2

                def down(eng, e=e, mt=mt, dhf=dhf, pd=pd, HD=HD, b33=b33, fr=list(psfree[pd])):
                    if mt == 0 and dhf == 0:
                        wfree(eng, [wload_cnt[3 * e + 2]])
                        W(eng, "v", HD)
                    wfree(eng, fr)
                    for fc in range(FC):
                        ins = eng.matmul(ps[pd][:, :],
                                         lhsT=h_fm[:, fc * 512 + mt * 128: fc * 512 + mt * 128 + 128],
                                         rhs=b33[:, fc * D + dhf * 512: fc * D + dhf * 512 + 512],
                                         start=(fc == 0), stop=(fc == FC - 1))
                    inc(ins, "t")
                onT(down)
                downT[(mt, dhf)] = CNT["t"]

                def devac(eng, e=e, mt=mt, dhf=dhf, pd=pd, bb=bb, hh=hh, fr=list(dv_free[mt])):
                    W(eng, "t", downT[(mt, dhf)])
                    if dhf == 0:
                        wfree(eng, fr)
                    be = bb * 32 + e
                    inc(eng.tensor_scalar(dv[mt][:, dhf * 512:(dhf + 1) * 512], ps[pd][:, :],
                                          ewT[:, hh * 64 + be: hh * 64 + be + 1], None, op0=OP.mult), "v")
                onV(devac)
                psfree[pd] = [("v", CNT["v"])]
                dvdone[mt] = CNT["v"]
        consumer_done[3 * e + 2] = downT[(3, 1)]
        down_prev = downT[(3, 1)]
        if 3 * e + 4 < 3 * E:
            emit_wload(3 * e + 4)

        for bb in range(2):
            for hh in range(2):
                mt = bb * 2 + hh

                def sct(eng, e=e, mt=mt, bb=bb, hh=hh, dvd=dvdone[mt], chain=sct_chain[bb]):
                    W(eng, "v", dvd)
                    if e == 0:
                        W(eng, "qzf", ZFILL_DONE)
                    W(eng, f"qc{bb}", chain)
                    be = bb * 32 + e
                    winc(eng.indirect_dma_start(
                        out=accf_d[:, :],
                        out_offset=bass.IndirectOffsetOnAxis(idxT_i[:, hh * 64 + be: hh * 64 + be + 1], 0),
                        in_=dv[mt][:, :], in_offset=None,
                        compute_op=OP.add), f"qc{bb}")
                if _PH >= 4:
                    onG(sct)
                sct_chain[bb] = CNT[f"qc{bb}"]
            dv_free[bb * 2] = [(f"qc{bb}", CNT[f"qc{bb}"])]
            dv_free[bb * 2 + 1] = [(f"qc{bb}", CNT[f"qc{bb}"])]

    SCT_ALL = [("qc0", CNT["qc0"]), ("qc1", CNT["qc1"])]
    MOE_S = CNT["s"]
    MOE_V = CNT["v"]

    accfree = {}
    for b in range(B):
        for t in range(NT):
            par = t % 2
            sp = par * 16
            prev = (b, t - 2) if t >= 2 else ((b - 1, NT + t - 2) if b > 0 else None)

            def fld(eng, b=b, t=t, par=par, prev=prev):
                if b == 0 and t == 0:
                    wfree(eng, SCT_ALL)
                if prev is not None:
                    wfree(eng, accfree[prev])
                wfree(eng, [x1st[(b, t)]])
                winc(eng.dma_start(out=acc_t[par][:, :], in_=acc_d[b, t * 128:(t + 1) * 128, :]), f"qx{par}")
                winc(eng.dma_start(out=x1f_t[par][:, :], in_=x1_d[b, t * 128:(t + 1) * 128, :]), f"qx{par}")
            onQ(fld)
            fq = CNT[f"qx{par}"]

            def fstat(eng, par=par, sp=sp, fq=fq, b=b, t=t):
                W(eng, f"qx{par}", fq)
                if b == 0 and t == 0:
                    W(eng, "s", MOE_S)
                    W(eng, "v", MOE_V)
                inc(ln_stats_v(eng, acc_t[par][:, :], sp, scrf[:, :]), "v")
            onV(fstat)
            vst = CNT["v"]

            def fsq(eng, sp=sp, vst=vst):
                W(eng, "v", vst)
                inc(eng.activation(out=stc[:, sp + 6:sp + 7], in_=stc[:, sp + 5:sp + 6],
                                   func=AF.Sqrt, bias=EPS), "s")
            onS(fsq)
            sst = CNT["s"]

            def frc(eng, sp=sp, sst=sst):
                W(eng, "s", sst)
                inc(ln_rstd_v(eng, sp), "v")
            onV(frc)
            vrc = CNT["v"]

            def fap(eng, par=par, sp=sp, vrc=vrc, fr=list(outtfree[par])):
                W(eng, "v", vrc)
                wfree(eng, fr)
                inc(eng.activation(out=out_t[par][:, :], in_=acc_t[par][:, :], func=AF.Identity,
                                   bias=stc[:, sp + 8:sp + 9], scale=stc[:, sp + 7:sp + 8]), "s")
            onS(fap)
            sap = CNT["s"]

            def fadd(eng, par=par, sap=sap):
                W(eng, "s", sap)
                inc(eng.tensor_add(out_t[par][:, :], out_t[par][:, :], x1f_t[par][:, :]), "v")
            onV(fadd)
            accfree[(b, t)] = [("v", CNT["v"]), ("s", sap)]
            vfin = CNT["v"]

            def fst(eng, b=b, t=t, par=par, vfin=vfin):
                W(eng, "v", vfin)
                winc(eng.dma_start(out=y_d[b, t * 128:(t + 1) * 128, :], in_=out_t[par][:, :]), f"qf{par}")
            onQ(fst)
            outtfree[par] = [(f"qf{par}", CNT[f"qf{par}"])]

    FIN = [("qf0", CNT["qf0"]), ("qf1", CNT["qf1"])]

    def fin(eng, FIN=FIN):
        wfree(eng, FIN)
    onQ(fin)

    block_ctx.__exit__(None, None, None)
    for c in reversed(list(sem_ctx.values())):
        c.__exit__(None, None, None)
    for c in reversed(ps_ctx):
        c.__exit__(None, None, None)
    arena_ctx.__exit__(None, None, None)
    return nc


def kernel(L0_attn_x, L0_attn_W_qk, L0_attn_b_qk, L0_ffn_router_w1, L0_ffn_router_w2,
           L0_ffn_w1, L0_ffn_w2, L0_ffn_w3):
    x = np.ascontiguousarray(L0_attn_x, dtype=np.float32)
    Wqk = np.asarray(L0_attn_W_qk, dtype=np.float32)
    bqk_in = np.asarray(L0_attn_b_qk, dtype=np.float32)
    Wcat = np.concatenate([Wqk[0::2], Wqk[1::2]], axis=0)
    wqkT = np.ascontiguousarray(Wcat.T.reshape(DC, 128, 2048).transpose(1, 0, 2))
    bq = bqk_in[0::2].reshape(8, 128).T
    bk = bqk_in[1::2].reshape(8, 128).T
    bqk = np.ascontiguousarray(np.concatenate([bq, bk], axis=1))
    rw1T = np.ascontiguousarray(np.asarray(L0_ffn_router_w1, dtype=np.float32)
                                .reshape(DC, 128, RH).transpose(1, 0, 2))
    rw2 = np.ascontiguousarray(L0_ffn_router_w2, dtype=np.float32)
    w1h = np.ascontiguousarray(np.asarray(L0_ffn_w1, dtype=np.float32)
                               .reshape(E, DC, 128, DFF).transpose(0, 2, 1, 3))
    w2h = np.ascontiguousarray(np.asarray(L0_ffn_w2, dtype=np.float32)
                               .reshape(E, DC, 128, DFF).transpose(0, 2, 1, 3))
    w3h = np.ascontiguousarray(np.asarray(L0_ffn_w3, dtype=np.float32)
                               .reshape(E, FC, 128, D).transpose(0, 2, 1, 3))
    idbf = np.eye(128, dtype=ml_dtypes.bfloat16)
    idf32 = np.eye(128, dtype=np.float32)

    nc = build_nc()
    in_maps = []
    for c in range(8):
        in_maps.append({
            "x": np.ascontiguousarray(x[2 * c:2 * c + 2]),
            "wqkT": wqkT, "bqk": bqk, "rw1T": rw1T, "rw2": rw2,
            "w1h": w1h, "w2h": w2h, "w3h": w3h,
            "idbf": idbf, "idf32": idf32,
        })
    res = run_bass_kernel_spmd(nc, in_maps[:_NCORES], core_ids=list(range(_NCORES)))
    out = np.concatenate([np.asarray(r["y"]) for r in res.results], axis=0)
    if _NCORES < 8:
        out = np.concatenate([out] + [out[0:2]] * (8 - _NCORES), axis=0)
    return np.ascontiguousarray(out.reshape(16, S, D).astype(np.float32))


if __name__ == "__main__":
    nc = build_nc()
    print("build ok; instructions:", sum(1 for _ in nc.all_instructions()))


# version 12
# speedup vs baseline: 1.0134x; 1.0134x over previous
import numpy as np
import ml_dtypes
import concourse.bass as bass
import concourse.mybir as mybir
from concourse.bass_utils import run_bass_kernel_spmd
from concourse import library_config

F32 = mybir.dt.float32
BF16 = mybir.dt.bfloat16
I32 = mybir.dt.int32
U32 = mybir.dt.uint32
I16 = mybir.dt.int16
AF = mybir.ActivationFunctionType
OP = mybir.AluOpType
AXX = mybir.AxisListType.X

B = 2
S = 2048
D = 1024
H = 16
DH = 64
HP = 8
E = 16
DFF = 2048
CAP = 256
RH = 128
NT = S // 128
NI = S // 512
DC = D // 128
FC = DFF // 128

SD = DH ** -0.5
LAM = 1.0507009873554804934193349852946
ALPHA = 1.6732632423543772848170429916717
CA = LAM * S ** -0.5
CB = LAM * ALPHA * S ** -0.5
EPS = 1e-5
NSPLIT = 2

_DBG = False
_PH = 4
_NCORES = 8
_ATT = 4
_SEVAC = 1
_SIMTP = 1
_NHP = 8


def build_nc():
    nc = bass.Bass("TRN2", target_bir_lowering=False)
    dk = "ExternalOutput" if _DBG else "Internal"

    x_h = nc.dram_tensor("x", [B, S, D], F32, kind="ExternalInput")
    wqk_h = nc.dram_tensor("wqkT", [128, DC, 2048], F32, kind="ExternalInput")
    bqk_h = nc.dram_tensor("bqk", [128, 16], F32, kind="ExternalInput")
    rw1_h = nc.dram_tensor("rw1T", [128, DC, RH], F32, kind="ExternalInput")
    rw2_h = nc.dram_tensor("rw2", [RH, E], F32, kind="ExternalInput")
    w1_h = nc.dram_tensor("w1h", [E, 128, DC, DFF], F32, kind="ExternalInput")
    w2_h = nc.dram_tensor("w2h", [E, 128, DC, DFF], F32, kind="ExternalInput")
    w3_h = nc.dram_tensor("w3h", [E, 128, FC, D], F32, kind="ExternalInput")
    idb_h = nc.dram_tensor("idbf", [128, 128], BF16, kind="ExternalInput")
    idf_h = nc.dram_tensor("idf32", [128, 128], F32, kind="ExternalInput")
    y_h = nc.dram_tensor("y", [B, S, D], F32, kind="ExternalOutput")
    x1d_h = nc.dram_tensor("x1d", [B, S, D], F32, kind=dk)
    ln2d_h = nc.dram_tensor("ln2d", [B, S, D], BF16, kind=dk)
    accd_h = nc.dram_tensor("accd", [B, S, D], F32, kind=dk)

    x_d, y_d = x_h.ap(), y_h.ap()
    x1_d, ln2_d, acc_d = x1d_h.ap(), ln2d_h.ap(), accd_h.ap()
    ln2f_d = ln2d_h.ap().rearrange("b s d -> (b s) d")
    accf_d = accd_h.ap().rearrange("b s d -> (b s) d")

    KB = 1024
    placements = []

    def place(name, shape, dtype, off):
        h = nc.virtual_sbuf_tensor(name, shape, dtype)
        placements.append((h, off))
        return h

    OA = 0
    wqk = place("wqk", [128, DC * 2048], BF16, OA)
    wb12_0 = place("wb12_0", [128, DC * DFF], BF16, OA)
    wb3_0 = place("wb3_0", [128, FC * D], BF16, OA)
    OB = 32 * KB
    k_fm = place("k_fm", [128, DC * 2048], BF16, OB)
    wb12_1 = place("wb12_1", [128, DC * DFF], BF16, OB)
    wb3_1 = place("wb3_1", [128, FC * D], BF16, OB)
    OC = 64 * KB
    q_fm = place("q_fm", [128, DC * 2048], BF16, OC)
    ei0 = place("ei0", [128, DC * 512], BF16, OC)
    ei1 = place("ei1", [128, DC * 512], BF16, OC + 8 * KB)
    h_fm = place("h_fm", [128, FC * 512], BF16, OC + 16 * KB)
    OD = 96 * KB
    ln1_tm = place("ln1_tm", [128, NT * D], BF16, OD)
    dv = [place(f"dv{i}", [128, D], F32, OD + i * 4 * KB) for i in range(4)]
    acc_t = [place(f"acct{i}", [128, D], F32, OD + 16 * KB + i * 4 * KB) for i in range(2)]
    x1f_t = [place(f"x1ft{i}", [128, D], F32, OD + 24 * KB + i * 4 * KB) for i in range(2)]
    OE = 128 * KB
    slab = place("slab", [128, DC * 512], BF16, OE)
    scrb = place("scrb", [128, D], BF16, OE)
    scrf = place("scrf", [128, D], F32, OE)
    ln_t = place("ln_t", [128, D], F32, OE + 4 * KB)
    sil_all = place("sil_all", [128, FC * 512], BF16, OE)
    out_t = [place("outt0", [128, D], F32, OE + 4 * KB),
             place("outt1", [128, D], F32, 136 * KB)]
    OF = 136 * KB
    out_fm = place("out_fm", [128, HP * 512], BF16, OF)
    ei_tm = place("ei_tm", [128, 4, D], BF16, 144 * KB)
    OG = 144 * KB
    eb = [place(f"eb{i}", [128, 2 * 512], BF16, OG + i * 2 * KB) for i in range(2)]
    apb = [place(f"apb{i}", [128, 2 * 512], BF16, OG + 4 * KB + i * 2 * KB) for i in range(2)]
    bpb = [place(f"bpb{i}", [128, 2 * 512], BF16, OG + 8 * KB + i * 2 * KB) for i in range(2)]
    OX = 156 * KB
    x_t = [place(f"xt{i}", [128, D], F32, OX + i * 4 * KB) for i in range(2)]
    OX1 = 164 * KB
    x1_t = [place(f"x1t{i}", [128, D], F32, OX1 + i * 4 * KB) for i in range(2)]
    OL = 172 * KB
    logits = place("logits", [64, S], F32, OL)
    zero_sb = place("zero_sb", [128, D], F32, OF)

    off = [180 * KB]

    def salloc(name, shape, dtype):
        per = int(np.prod(shape[1:])) * mybir.dt.size(dtype)
        o = (off[0] + 31) // 32 * 32
        h = place(name, shape, dtype, o)
        off[0] = o + per
        return h, o

    id_bf, _ = salloc("id_bf", [128, 128], BF16)
    id_f, _ = salloc("id_f", [128, 128], F32)
    bqk, _ = salloc("bqk_sb", [128, 16], F32)
    rw2, _ = salloc("rw2_sb", [RH, E], BF16)
    rw1, _ = salloc("rw1_sb", [128, DC * RH], BF16)
    stc, _ = salloc("stc", [128, 32], F32)
    kbar, _ = salloc("kbar", [128, DC], F32)
    colsv, _ = salloc("colsv", [128, DC], F32)
    corr, _ = salloc("corr", [128, DC], F32)
    tmpc, _ = salloc("tmpc", [128, 2], F32)
    r1bf, _ = salloc("r1bf", [128, 128], BF16)
    st2, _ = salloc("st2", [64, 8], F32)
    vals, OVALS = salloc("vals", [64, CAP], F32)
    idx, _ = salloc("idx", [64, CAP], U32)
    idxf, OIDXF = salloc("idxf", [64, CAP], F32)
    idxT_i, _ = salloc("idxT_i", [128, 128], U32)
    idxT_h, _ = salloc("idxT_h", [128, 128], I16)
    ewT, _ = salloc("ewT", [128, 128], F32)
    idxg, OIDXG = salloc("idxg", [128, 1024], I16)
    cst, _ = salloc("cst", [128, 8], F32)
    out_tm = place("out_tm", [128, D], BF16, OVALS)
    ln2_t = place("ln2_t", [128, D], BF16, OIDXF)
    ln2f_t = place("ln2f_t", [128, D], BF16, OIDXG)
    assert off[0] <= 192 * KB, off[0]

    arena_ctx = nc.sbuf_tensor("arena", [128, 192 * KB], mybir.dt.uint8)
    arena_h = arena_ctx.__enter__()
    base = nc.lookup_mloc(arena_h).addr
    for h, o in placements:
        nc._tensor(h.name, h.shape, h.dtype, type="SB")
        ml = nc.lookup_mloc(h)
        ml.allocated = True
        ml.addr = base + o

    ps_ctx = [nc.psum_tensor(f"ps{i}", [128, 512], F32) for i in range(6)]
    ps = [c.__enter__() for c in ps_ctx]
    ps6_ctx = nc.psum_tensor("ps6b", [128, 1024], BF16)
    psb6 = ps6_ctx.__enter__()
    ps_ctx.append(ps6_ctx)
    ps7_ctx = nc.psum_tensor("ps7", [128, 512], F32)
    ps7 = ps7_ctx.__enter__()
    ps_ctx.append(ps7_ctx)
    ps = ps + [None, ps7]

    sem_ctx = {}
    sems = {}
    for nme in ["qcst", "qx0", "qx1", "qzf", "qs0", "qs1", "qlns", "qidx", "qwp",
                "qw0", "qw1", "qw2", "qg0", "qg1", "qc0", "qc1", "qf0", "qf1",
                "t", "v", "s", "vv"]:
        c = nc.semaphore(f"sem_{nme}")
        sem_ctx[nme] = c
        sems[nme] = c.__enter__()
    CNT = {k: 0 for k in sems}

    block_ctx = nc.Block()
    block = block_ctx.__enter__()

    def inc(instr, nme, k=1):
        instr.then_inc(sems[nme], k)
        CNT[nme] += k
        return CNT[nme]

    def winc(instr, nme):
        return inc(instr, nme, 16)

    def W(eng, nme, val):
        if val > 0:
            eng.wait_ge(sems[nme], val)

    def wfree(eng, conds):
        if isinstance(conds, tuple):
            conds = [conds]
        for nme, v in conds:
            W(eng, nme, v)

    def onT(f):
        block.tensor(f)

    def onV(f):
        block.vector(f)

    def onS(f):
        block.scalar(f)

    def onQ(f):
        block.sync(f)

    def onG(f):
        block.gpsimd(f)

    def vnext(eng, instr):
        inc(instr, "vv")
        W(eng, "vv", CNT["vv"])

    def ln_stats_v(eng, src, sp, dump):
        eng.reduce_sum(stc[:, sp:sp + 1], src, axis=AXX)
        vnext(eng, eng.scalar_tensor_tensor(out=dump, in0=src, scalar=1.0, in1=src,
                                            op0=OP.mult, op1=OP.mult, accum_out=stc[:, sp + 1:sp + 2]))
        eng.tensor_scalar(stc[:, sp + 2:sp + 3], stc[:, sp:sp + 1], 1.0 / D, None, op0=OP.mult)
        vnext(eng, eng.tensor_scalar(stc[:, sp + 3:sp + 4], stc[:, sp + 1:sp + 2], 1.0 / D, None, op0=OP.mult))
        vnext(eng, eng.tensor_tensor(stc[:, sp + 4:sp + 5], stc[:, sp + 2:sp + 3], stc[:, sp + 2:sp + 3], op=OP.mult))
        return eng.tensor_sub(stc[:, sp + 5:sp + 6], stc[:, sp + 3:sp + 4], stc[:, sp + 4:sp + 5])

    def ln_rstd_v(eng, sp):
        return eng.reciprocal(stc[:, sp + 7:sp + 8], stc[:, sp + 6:sp + 7])

    def ln_apply_v(eng, out_ap, in_ap, sp):
        return eng.tensor_scalar(out_ap, in_ap, stc[:, sp + 2:sp + 3], stc[:, sp + 7:sp + 8],
                                 op0=OP.subtract, op1=OP.mult)

    def prelude_q(eng):
        winc(eng.dma_start(out=id_bf[:, :], in_=idb_h.ap()[:, :]), "qcst")
        winc(eng.dma_start(out=id_f[:, :], in_=idf_h.ap()[:, :]), "qcst")
        winc(eng.dma_start(out=bqk[:, :], in_=bqk_h.ap()[:, :]), "qcst")
    onQ(prelude_q)
    CONSTS_Q = CNT["qcst"]

    def prelude_g(eng):
        winc(eng.dma_start(out=wqk[:, :], in_=wqk_h.ap()[:, :, :]), "qwp")
        winc(eng.dma_start(out=rw1[:, :], in_=rw1_h.ap()[:, :, :]), "qwp")
        winc(eng.dma_start(out=rw2[:, :], in_=rw2_h.ap()[:, :]), "qwp")
    onG(prelude_g)
    WQK_LOADED = CNT["qwp"]

    CONSTS = [EPS, CA]
    for ci, cv in enumerate(CONSTS):
        nc.const_aps.register(F32, cv, cst[:, ci:ci + 1])

    def prelude_v(eng):
        eng.memset(zero_sb[:, :], 0.0)
        eng.memset(logits[:, :], 0.0)
        for ci, cv in enumerate(CONSTS):
            eng.memset(cst[:, ci:ci + 1], cv)
        vnext(eng, eng.memset(colsv[:, :], 0.0))
        inc(eng.memset(tmpc[:, 1:2], 0.0), "v")
    onV(prelude_v)
    ZMEM = CNT["v"]

    def zfill(eng):
        W(eng, "v", ZMEM)
        for b in range(B):
            for t in range(NT):
                winc(eng.dma_start(out=acc_d[b, t * 128:(t + 1) * 128, :], in_=zero_sb[:, :]), "qzf")
    onQ(zfill)
    ZFILL_DONE = CNT["qzf"]

    xfree = [[], []]
    x1free = [[], []]
    ln2free = []
    out_tm_free = []
    ln2f_free = []
    outtfree = [[], []]
    psfree = [[("v", 0)]] * 8
    psfree = [[] for _ in range(8)]
    ebfree = [[], []]
    apfree = [[], []]
    LNST = {}
    lnst = [0, 0]
    x1st = {}
    G = {"proj_i": 0, "a6T": 0, "a6V": 0, "slabsum": 0}

    def attention(b):
        nonlocal ln2free, out_tm_free, ln2f_free
        for i in range(NI):
            appl = [0] * 4
            for tc in range(4):
                t = i * 4 + tc
                par = t % 2
                sp = par * 16

                def ld(eng, b=b, t=t, par=par, fr=list(xfree[par])):
                    wfree(eng, fr)
                    winc(eng.dma_start(out=x_t[par][:, :], in_=x_d[b, t * 128:(t + 1) * 128, :]), f"qx{par}")
                onQ(ld)
                xq = CNT[f"qx{par}"]

                def stats(eng, par=par, sp=sp, xq=xq, fr=list(x1free[0]), b=b, t=t):
                    W(eng, f"qx{par}", xq)
                    if b == 0 and t == 0:
                        W(eng, "qcst", CONSTS_Q)
                    wfree(eng, fr)
                    inc(ln_stats_v(eng, x_t[par][:, :], sp, x1_t[0][:, :]), "v")
                onV(stats)
                vst = CNT["v"]

                def rstd(eng, sp=sp, vst=vst):
                    W(eng, "v", vst)
                    inc(eng.activation(out=stc[:, sp + 6:sp + 7], in_=stc[:, sp + 5:sp + 6],
                                       func=AF.Sqrt, bias=EPS), "s")
                onS(rstd)
                sst = CNT["s"]

                def rcpb(eng, sp=sp, sst=sst, par=par, t=t):
                    W(eng, "s", sst)
                    vnext(eng, ln_rstd_v(eng, sp))
                    inc(ln_apply_v(eng, ln1_tm[:, t * D:(t + 1) * D], x_t[par][:, :], sp), "v")
                onV(rcpb)
                appl[tc] = ("v", CNT["v"])
                xfree[par] = [("v", CNT["v"])]

            for tc in range(4):
                t = i * 4 + tc
                for dq in range(2):
                    hb = dq % 2

                    def tpose(eng, t=t, dq=dq, hb=hb, ap4=appl[tc], fr=list(psfree[6 + hb])):
                        wfree(eng, [ap4])
                        wfree(eng, fr)
                        for c in range(4):
                            dc = dq * 4 + c
                            ins = eng.transpose(psb6[:, hb * 512 + c * 128: hb * 512 + (c + 1) * 128],
                                                ln1_tm[:, t * D + dc * 128: t * D + dc * 128 + 128],
                                                id_bf[:, :])
                        inc(ins, "t")
                    onT(tpose)
                    tdone = CNT["t"]

                    def tevac(eng, tc=tc, dq=dq, hb=hb, tdone=tdone,
                              pj=G["proj_i"], ss=G["slabsum"], av=G["a6V"]):
                        W(eng, "t", tdone)
                        W(eng, "t", pj)
                        W(eng, "v", ss)
                        W(eng, "v", av)
                        for c in range(4):
                            dc = dq * 4 + c
                            ins = eng.activation(out=slab[:, dc * 512 + tc * 128: dc * 512 + tc * 128 + 128],
                                                 in_=psb6[:, hb * 512 + c * 128: hb * 512 + (c + 1) * 128], func=AF.Copy)
                        inc(ins, "s")
                    onS(tevac)
                    psfree[6 + hb] = [("s", CNT["s"])]
            slab_ready = ("s", CNT["s"])

            def slabsum(eng, sr=slab_ready):
                wfree(eng, [sr])
                for dc in range(DC):
                    vnext(eng, eng.reduce_sum(tmpc[:, 0:1], slab[:, dc * 512:(dc + 1) * 512], axis=AXX))
                    ins = eng.tensor_add(colsv[:, dc:dc + 1], colsv[:, dc:dc + 1], tmpc[:, 0:1])
                    if dc < DC - 1:
                        vnext(eng, ins)
                inc(ins, "v")
            onV(slabsum)
            G["slabsum"] = CNT["v"]

            for fc in (range(16) if _ATT >= 2 else []):
                bank = (i * 16 + fc) % 2

                def proj(eng, fc=fc, bank=bank, sr=slab_ready, i=i, fr=list(psfree[bank])):
                    wfree(eng, [sr])
                    if i == 0 and fc == 0:
                        W(eng, "qwp", WQK_LOADED)
                    wfree(eng, fr)
                    for dc in range(DC):
                        ins = eng.matmul(ps[bank][:, :],
                                         lhsT=wqk[:, dc * 2048 + fc * 128: dc * 2048 + fc * 128 + 128],
                                         rhs=slab[:, dc * 512:(dc + 1) * 512],
                                         start=(dc == 0), stop=(dc == DC - 1))
                    inc(ins, "t")
                onT(proj)
                pj = CNT["t"]

                def pevac(eng, fc=fc, bank=bank, i=i, pj=pj):
                    W(eng, "t", pj)
                    dst = q_fm if fc < 8 else k_fm
                    c = fc if fc < 8 else fc - 8
                    inc(eng.tensor_scalar(dst[:, c * 2048 + i * 512: c * 2048 + i * 512 + 512],
                                          ps[bank][:, :], bqk[:, fc:fc + 1], None, op0=OP.add), "v")
                onV(pevac)
                psfree[bank] = [("v", CNT["v"])]
            G["proj_i"] = CNT["t"]
        PROJ_EVAC = CNT["v"]

        def kcenter(eng, pe=PROJ_EVAC):
            W(eng, "v", pe)
            W(eng, "vv", CNT["vv"])
            for dc in range(DC):
                vnext(eng, eng.reduce_sum(kbar[:, dc:dc + 1], k_fm[:, dc * 2048:(dc + 1) * 2048], axis=AXX))
                vnext(eng, eng.tensor_scalar(kbar[:, dc:dc + 1], kbar[:, dc:dc + 1], 1.0 / S, None, op0=OP.mult))
                eng.tensor_scalar(k_fm[:, dc * 2048:(dc + 1) * 2048], k_fm[:, dc * 2048:(dc + 1) * 2048],
                                  kbar[:, dc:dc + 1], SD, op0=OP.subtract, op1=OP.mult)
            vnext(eng, eng.tensor_scalar(corr[:, :], colsv[:, :], CB, None, op0=OP.mult))
            inc(eng.memset(colsv[:, :], 0.0), "v")
        if _ATT >= 2:
            onV(kcenter)
        KC = CNT["v"]

        g = 0
        simdone = {}
        expdone = {}
        aprdone = {}
        av_last = 0
        for i in range(NI):
            for hp in range(HP if _ATT >= 4 else _NHP):
                for j in (range(NT + 1) if _ATT >= 3 else []):
                    if j < NT:
                        pA, pB = g % 2, 2 + g % 2

                        def sim(eng, i=i, hp=hp, j=j, g=g, pA=pA, pB=pB, kc=KC,
                                frA=list(psfree[pA]), frB=list(psfree[pB])):
                            if g == 0:
                                W(eng, "v", kc)
                            if g >= 2:
                                W(eng, "s", expdone[g - 2])
                                W(eng, "v", aprdone[g - 2])
                            else:
                                wfree(eng, frA)
                                wfree(eng, frB)
                            if _SIMTP:
                                eng.matmul(ps[pA][:, :],
                                           lhsT=k_fm[0:64, hp * 2048 + j * 128: hp * 2048 + j * 128 + 128],
                                           rhs=q_fm[0:64, hp * 2048 + i * 512: hp * 2048 + i * 512 + 512],
                                           start=True, stop=True, tile_position=(0, 0))
                                inc(eng.matmul(ps[pB][:, :],
                                               lhsT=k_fm[64:128, hp * 2048 + j * 128: hp * 2048 + j * 128 + 128],
                                               rhs=q_fm[64:128, hp * 2048 + i * 512: hp * 2048 + i * 512 + 512],
                                               start=True, stop=True, tile_position=(64, 0)), "t")
                            else:
                                eng.matmul(ps[pA][:, :],
                                           lhsT=k_fm[:, hp * 2048 + j * 128: hp * 2048 + j * 128 + 128],
                                           rhs=q_fm[:, hp * 2048 + i * 512: hp * 2048 + i * 512 + 512],
                                           start=True, stop=True)
                                inc(eng.matmul(ps[pB][:, :],
                                               lhsT=k_fm[:, hp * 2048 + j * 128: hp * 2048 + j * 128 + 128],
                                               rhs=q_fm[:, hp * 2048 + i * 512: hp * 2048 + i * 512 + 512],
                                               start=True, stop=True), "t")
                        onT(sim)
                        simdone[g] = CNT["t"]
                        s_does_a = (j % 8) < NSPLIT
                        ep = g % 2

                        def sevac(eng, g=g, pA=pA, pB=pB, ep=ep, s_does_a=s_does_a,
                                  fre=list(ebfree[ep]), fra=list(apfree[ep])):
                            W(eng, "t", simdone[g])
                            wfree(eng, fre)
                            fE = AF.Exp if _SEVAC else AF.Copy
                            eng.activation(out=eb[ep][:, 0:512], in_=ps[pA][:, :], func=fE)
                            ins = eng.activation(out=eb[ep][:, 512:1024], in_=ps[pB][:, :], func=fE)
                            if s_does_a:
                                wfree(eng, fra)
                                if _SEVAC:
                                    eng.activation(out=apb[ep][:, 0:512], in_=ps[pA][:, :], func=AF.Relu, scale=CA)
                                    ins = eng.activation(out=apb[ep][:, 512:1024], in_=ps[pB][:, :], func=AF.Relu, scale=CA)
                                else:
                                    eng.activation(out=apb[ep][:, 0:512], in_=ps[pA][:, :], func=AF.Copy)
                                    ins = eng.activation(out=apb[ep][:, 512:1024], in_=ps[pB][:, :], func=AF.Copy)
                            inc(ins, "s")
                        onS(sevac)
                        expdone[g] = CNT["s"]

                        def vevac(eng, g=g, pA=pA, pB=pB, ep=ep, s_does_a=s_does_a,
                                  fra=list(apfree[ep])):
                            if not s_does_a:
                                W(eng, "t", simdone[g])
                                W(eng, "s", expdone[g])
                                wfree(eng, fra)
                                if _SEVAC:
                                    eng.tensor_scalar(apb[ep][:, 0:512], ps[pA][:, :], 0.0, CA, op0=OP.max, op1=OP.mult)
                                    eng.tensor_scalar(apb[ep][:, 512:1024], ps[pB][:, :], 0.0, CA, op0=OP.max, op1=OP.mult)
                                else:
                                    eng.tensor_copy(apb[ep][:, 0:512], ps[pA][:, :])
                                    eng.tensor_copy(apb[ep][:, 512:1024], ps[pB][:, :])
                            W(eng, "s", expdone[g])
                            if _SEVAC:
                                eng.tensor_scalar(bpb[ep][:, 0:512], eb[ep][:, 0:512], 1.0, CB, op0=OP.min, op1=OP.mult)
                                inc(eng.tensor_scalar(bpb[ep][:, 512:1024], eb[ep][:, 512:1024], 1.0, CB,
                                                      op0=OP.min, op1=OP.mult), "v")
                            else:
                                eng.tensor_copy(bpb[ep][:, 0:512], eb[ep][:, 0:512])
                                inc(eng.tensor_copy(bpb[ep][:, 512:1024], eb[ep][:, 512:1024]), "v")
                        onV(vevac)
                        aprdone[g] = CNT["v"]
                        ebfree[ep] = [("v", CNT["v"])]
                        psfree[pA] = [("s", expdone[g]), ("v", aprdone[g])]
                        psfree[pB] = [("s", expdone[g]), ("v", aprdone[g])]
                    if j > 0 and _ATT >= 4:
                        ja = j - 1
                        ga = g - 1
                        ep = ga % 2

                        def attnv(eng, hp=hp, ja=ja, ga=ga, ep=ep,
                                  fr4=list(psfree[4]), fr5=list(psfree[5])):
                            W(eng, "v", aprdone[ga])
                            W(eng, "s", expdone[ga])
                            if ja == 0:
                                wfree(eng, fr4)
                                wfree(eng, fr5)
                            vA = ln1_tm[:, ja * D + hp * 128: ja * D + hp * 128 + 64]
                            vB = ln1_tm[:, ja * D + hp * 128 + 64: ja * D + hp * 128 + 128]
                            st = (ja == 0)
                            sp_ = (ja == NT - 1)
                            eng.matmul(ps[4][0:64, :], lhsT=vA, rhs=apb[ep][:, 0:512],
                                       start=st, stop=False, tile_position=(0, 0))
                            eng.matmul(ps[4][0:64, :], lhsT=vA, rhs=bpb[ep][:, 0:512],
                                       start=False, stop=sp_, tile_position=(0, 0))
                            eng.matmul(ps[5][64:128, :], lhsT=vB, rhs=apb[ep][:, 512:1024],
                                       start=st, stop=False, tile_position=(0, 64))
                            inc(eng.matmul(ps[5][64:128, :], lhsT=vB, rhs=bpb[ep][:, 512:1024],
                                           start=False, stop=sp_, tile_position=(0, 64)), "t")
                        onT(attnv)
                        av_last = CNT["t"]
                        apfree[ep] = [("t", CNT["t"])]
                    if j < NT:
                        g += 1

                def oevac(eng, hp=hp, i=i, av=av_last, a6T=G["a6T"], b=b):
                    W(eng, "t", av)
                    W(eng, "t", a6T)
                    if b == 0 and i == 0:
                        W(eng, "qzf", ZFILL_DONE)
                    eng.tensor_scalar(out_fm[0:64, hp * 512:(hp + 1) * 512], ps[4][0:64, :],
                                      corr[0:64, hp:hp + 1], None, op0=OP.subtract)
                    inc(eng.tensor_scalar(out_fm[64:128, hp * 512:(hp + 1) * 512], ps[5][64:128, :],
                                          corr[64:128, hp:hp + 1], None, op0=OP.subtract), "v")
                if _ATT >= 4:
                    onV(oevac)
                    psfree[4] = [("v", CNT["v"])]
                    psfree[5] = [("v", CNT["v"])]
            OUTE = CNT["v"]

            for tc in range(4):
                t = i * 4 + tc
                par = t % 2
                sp = par * 16

                def ldx(eng, b=b, t=t, par=par, fr=list(xfree[par])):
                    wfree(eng, fr)
                    winc(eng.dma_start(out=x_t[par][:, :], in_=x_d[b, t * 128:(t + 1) * 128, :]), f"qx{par}")
                onQ(ldx)
                xq = CNT[f"qx{par}"]

                ev = 0
                for dq in range(2):
                    hb = dq % 2

                    def a6t(eng, tc=tc, dq=dq, hb=hb, oe=OUTE, fr=list(psfree[6 + hb])):
                        W(eng, "v", oe)
                        wfree(eng, fr)
                        for c in range(4):
                            dc = dq * 4 + c
                            ins = eng.transpose(psb6[:, hb * 512 + c * 128: hb * 512 + (c + 1) * 128],
                                                out_fm[:, dc * 512 + tc * 128: dc * 512 + tc * 128 + 128],
                                                id_bf[:, :])
                        inc(ins, "t")
                    onT(a6t)
                    td = CNT["t"]

                    def a6e(eng, dq=dq, hb=hb, td=td, fr=list(out_tm_free)):
                        W(eng, "t", td)
                        if dq == 0:
                            wfree(eng, fr)
                        inc(eng.activation(out=out_tm[:, dq * 512:(dq + 1) * 512],
                                           in_=psb6[:, hb * 512: hb * 512 + 512], func=AF.Copy), "s")
                    onS(a6e)
                    psfree[6 + hb] = [("s", CNT["s"])]
                    ev = CNT["s"]

                def a6v(eng, sp=sp, ev=ev):
                    W(eng, "s", ev)
                    inc(ln_stats_v(eng, out_tm[:, :], sp, scrb[:, :]), "v")
                onV(a6v)
                out_tm_free = [("v", CNT["v"])]
                vst = CNT["v"]

                def a6s(eng, sp=sp, vst=vst):
                    W(eng, "v", vst)
                    inc(eng.activation(out=stc[:, sp + 6:sp + 7], in_=stc[:, sp + 5:sp + 6],
                                       func=AF.Sqrt, bias=EPS), "s")
                onS(a6s)
                sst = CNT["s"]

                def a6r(eng, sp=sp, sst=sst):
                    W(eng, "s", sst)
                    vnext(eng, ln_rstd_v(eng, sp))
                    inc(ln_apply_v(eng, ln_t[:, :], out_tm[:, :], sp), "v")
                onV(a6r)
                out_tm_free.append(("v", CNT["v"]))
                sap = CNT["v"]

                def a6add(eng, par=par, sp=sp, xq=xq, sap=sap, fr=list(x1free[par])):
                    W(eng, f"qx{par}", xq)
                    W(eng, "v", sap)
                    wfree(eng, fr)
                    vnext(eng, eng.tensor_add(x1_t[par][:, :], ln_t[:, :], x_t[par][:, :]))
                    inc(ln_stats_v(eng, x1_t[par][:, :], sp, scrf[:, :]), "v")
                onV(a6add)
                xfree[par] = [("v", CNT["v"])]
                vadd = CNT["v"]

                def x1store(eng, b=b, t=t, par=par, vadd=vadd):
                    W(eng, "v", vadd)
                    winc(eng.dma_start(out=x1_d[b, t * 128:(t + 1) * 128, :], in_=x1_t[par][:, :]), f"qs{par}")
                onQ(x1store)
                x1st[(b, t)] = (f"qs{par}", CNT[f"qs{par}"])

                def l2s(eng, sp=sp, vadd=vadd):
                    W(eng, "v", vadd)
                    inc(eng.activation(out=stc[:, sp + 6:sp + 7], in_=stc[:, sp + 5:sp + 6],
                                       func=AF.Sqrt, bias=EPS), "s")
                onS(l2s)
                sst2 = CNT["s"]

                def l2r(eng, sp=sp, sst2=sst2, par=par, fr=list(ln2free)):
                    W(eng, "s", sst2)
                    wfree(eng, fr)
                    vnext(eng, ln_rstd_v(eng, sp))
                    inc(ln_apply_v(eng, ln2_t[:, :], x1_t[par][:, :], sp), "v")
                onV(l2r)
                x1free[par] = [("v", CNT["v"]), x1st[(b, t)]]
                sl2 = CNT["v"]

                def l2store(eng, b=b, t=t, sl2=sl2):
                    W(eng, "v", sl2)
                    winc(eng.dma_start(out=ln2_d[b, t * 128:(t + 1) * 128, :], in_=ln2_t[:, :]), "qlns")
                onQ(l2store)
                lnst[b] = CNT["qlns"]
                ln2free = [("qlns", CNT["qlns"])]

                for dq in range(2):
                    hb = dq % 2

                    def l2t(eng, dq=dq, hb=hb, sl2=sl2, fr=list(psfree[6 + hb])):
                        W(eng, "v", sl2)
                        wfree(eng, fr)
                        for c in range(4):
                            dc = dq * 4 + c
                            ins = eng.transpose(psb6[:, hb * 512 + c * 128: hb * 512 + (c + 1) * 128],
                                                ln2_t[:, dc * 128:(dc + 1) * 128], id_bf[:, :])
                        inc(ins, "t")
                    onT(l2t)
                    td = CNT["t"]

                    def l2te(eng, dq=dq, hb=hb, td=td, fr=list(ln2f_free)):
                        W(eng, "t", td)
                        if dq == 0:
                            wfree(eng, fr)
                        for c in range(4):
                            dc = dq * 4 + c
                            ins = eng.activation(out=ln2f_t[:, dc * 128:(dc + 1) * 128],
                                                 in_=psb6[:, hb * 512 + c * 128: hb * 512 + (c + 1) * 128], func=AF.Copy)
                        inc(ins, "s")
                    onS(l2te)
                    psfree[6 + hb] = [("s", CNT["s"])]
                ln2free.append(("t", 0))
                L2F = ("s", CNT["s"])

                prevq = (b, t - 4) if t >= 4 else ((0, t + 12) if b == 1 else None)

                def router(eng, L2F=L2F, t=t, prevq=prevq):
                    wfree(eng, [L2F])
                    if prevq is not None:
                        W(eng, "v", LNST[prevq])
                    tq = t % 4
                    for dc in range(DC):
                        ins = eng.matmul(ps7[:, tq * 128:(tq + 1) * 128], lhsT=rw1[:, dc * RH:(dc + 1) * RH],
                                         rhs=ln2f_t[:, dc * 128:(dc + 1) * 128],
                                         start=(dc == 0), stop=(dc == DC - 1))
                    inc(ins, "t")
                onT(router)
                rt = CNT["t"]
                ln2free[-1] = ("t", rt)
                ln2f_free = [("t", rt)]

                def rsilu(eng, rt=rt, t=t):
                    W(eng, "t", rt)
                    inc(eng.activation(out=r1bf[:, :], in_=ps7[:, (t % 4) * 128:(t % 4) * 128 + 128], func=AF.Silu), "s")
                onS(rsilu)
                rs = CNT["s"]

                def rlog(eng, rs=rs, b=b, t=t):
                    W(eng, "s", rs)
                    tp = (0, 0) if b == 0 else (0, 32)
                    r0 = 0 if b == 0 else 32
                    inc(eng.matmul(ps7[r0:r0 + 16, (t % 4) * 128:(t % 4) * 128 + 128],
                                   lhsT=rw2[:, :], rhs=r1bf[:, :], start=True, stop=True,
                                   tile_position=tp), "t")
                onT(rlog)
                lg = CNT["t"]

                def rlev(eng, lg=lg, b=b, t=t):
                    W(eng, "t", lg)
                    r0 = 0 if b == 0 else 32
                    inc(eng.tensor_copy(out=logits[b * 32:b * 32 + 16, t * 128:(t + 1) * 128],
                                        in_=ps7[r0:r0 + 16, (t % 4) * 128:(t % 4) * 128 + 128]), "v")
                onV(rlev)
                LNST[(b, t)] = CNT["v"]
            G["a6T"] = CNT["t"]
            G["a6V"] = CNT["v"]
        return CNT["t"]

    attention(0)
    PROJ_B1_MARK = None
    attention(1)
    PROJ_B1 = G["proj_i"]
    SIM_B1 = CNT["t"]
    ATT_S = CNT["s"]
    ATT_V = CNT["v"]

    tkv = CNT["v"]

    def topk1(eng):
        W(eng, "v", ATT_V)
        vnext(eng, eng.reduce_max(st2[:, 0:1], logits[:, :], axis=AXX))
        inc(eng.tensor_scalar(st2[:, 1:2], st2[:, 0:1], -1.0, None, op0=OP.mult), "v")
    if _PH >= 2:
        onV(topk1)
    tk1 = CNT["v"]

    def texp(eng, tk1=tk1):
        W(eng, "v", tk1)
        inc(eng.activation(out=logits[:, :], in_=logits[:, :], func=AF.Exp, bias=st2[:, 1:2]), "s")
    if _PH >= 2:
        onS(texp)
    tke = CNT["s"]

    def trounds(eng, tke=tke, l1=lnst[1]):
        W(eng, "s", tke)
        W(eng, "qlns", l1)
        eng.memset(idxg[:, :], 0)
        vnext(eng, eng.reduce_sum(st2[:, 2:3], logits[:, :], axis=AXX))
        vnext(eng, eng.reciprocal(st2[:, 3:4], st2[:, 2:3]))
        for r in range(32):
            vnext(eng, eng.max(out=vals[:, r * 8:(r + 1) * 8], in_=logits[:, :]))
            vnext(eng, eng.max_index(out=idx[:, r * 8:(r + 1) * 8], in_max=vals[:, r * 8:(r + 1) * 8],
                                     in_values=logits[:, :]))
            vnext(eng, eng.match_replace(out=logits[:, :], in_to_replace=vals[:, r * 8:(r + 1) * 8],
                                         in_values=logits[:, :], imm_value=-1.0))
        vnext(eng, eng.tensor_scalar(vals[:, :], vals[:, :], st2[:, 3:4], None, op0=OP.mult))
        vnext(eng, eng.reduce_sum(st2[:, 4:5], vals[:, :], axis=AXX))
        vnext(eng, eng.tensor_scalar(st2[:, 4:5], st2[:, 4:5], 1e-9, None, op0=OP.add))
        vnext(eng, eng.reciprocal(st2[:, 5:6], st2[:, 4:5]))
        vnext(eng, eng.tensor_scalar(vals[:, :], vals[:, :], st2[:, 5:6], None, op0=OP.mult))
        vnext(eng, eng.tensor_copy(out=idxf[:, :], in_=idx[:, :]))
        inc(eng.tensor_scalar(idxf[32:64, :], idxf[32:64, :], 2048.0, None, op0=OP.add), "v")
    if _PH >= 2:
        onV(trounds)
    tkr = CNT["v"]

    def ttrans(eng, tkr=tkr):
        W(eng, "v", tkr)
        eng.transpose(ps7[:, 0:64], idxf[0:64, 0:128], id_f[0:64, 0:64])
        eng.transpose(ps7[:, 64:128], idxf[0:64, 128:256], id_f[0:64, 0:64])
        eng.transpose(ps7[:, 128:192], vals[0:64, 0:128], id_f[0:64, 0:64])
        inc(eng.transpose(ps7[:, 192:256], vals[0:64, 128:256], id_f[0:64, 0:64]), "t")
    if _PH >= 2:
        onT(ttrans)
    tkt = CNT["t"]

    def tevac2(eng, tkt=tkt):
        W(eng, "t", tkt)
        eng.tensor_copy(out=idxT_i[:, 0:64], in_=ps7[:, 0:64])
        eng.tensor_copy(out=idxT_i[:, 64:128], in_=ps7[:, 64:128])
        eng.tensor_copy(out=ewT[:, 0:64], in_=ps7[:, 128:192])
        inc(eng.tensor_copy(out=ewT[:, 64:128], in_=ps7[:, 192:256]), "v")
    if _PH >= 2:
        onV(tevac2)
        psfree[6] = [("v", CNT["v"])]
    tkv = CNT["v"]

    IDXG_DONE = 0

    wb12 = [wb12_0, wb12_1]
    wb3 = [wb3_0, wb3_1]
    consumer_done = {}
    wload_cnt = {}

    def emit_wload(L):
        e, m = divmod(L, 3)
        buf = L % 2
        src = [w1_h, w2_h, w3_h][m]

        def wl(eng, e=e, m=m, buf=buf, L=L):
            if L == 0:
                W(eng, "t", PROJ_B1)
            elif L == 1:
                W(eng, "t", SIM_B1)
            else:
                W(eng, "t", consumer_done[L - 2])
            dst = wb12[buf] if m < 2 else wb3[buf]
            winc(eng.dma_start(out=dst[:, :], in_=src.ap()[e, :, :, :]), f"qw{L % 3}")
        onG(wl)
        wload_cnt[L] = (f"qw{L % 3}", CNT[f"qw{L % 3}"])

    eib = [ei0, ei1]
    gath_cnt = {}
    eiT_done = {}
    eiT_evac = {}
    eifm_free = {}

    def emit_gather(e):
        def gth(eng, e=e, prev=eiT_done.get(e - 1, 0)):
            if e == 0:
                W(eng, "t", SIM_B1)
                W(eng, "qlns", lnst[1])
                W(eng, "v", tkv)
            else:
                W(eng, "t", prev)
            for mt in range(4):
                bb, hh = divmod(mt, 2)
                be = bb * 32 + e
                winc(eng.indirect_dma_start(
                    out=ei_tm[:, mt, :],
                    out_offset=None,
                    in_=ln2f_d[:, :],
                    in_offset=bass.IndirectOffsetOnAxis(idxT_i[:, hh * 64 + be: hh * 64 + be + 1], 0)),
                    f"qg{e % 2}")
        onG(gth)
        gath_cnt[e] = (f"qg{e % 2}", CNT[f"qg{e % 2}"])

    def emit_eitrans(e):
        gq = gath_cnt[e]
        for mt in range(4):
            for dq in range(2):
                hb = dq % 2

                def eit(eng, e=e, mt=mt, dq=dq, hb=hb, gq=gq, fr=list(psfree[6 + hb])):
                    wfree(eng, [gq])
                    wfree(eng, fr)
                    if mt == 0 and dq == 0:
                        wfree(eng, eifm_free.get(e, []))
                    for c in range(4):
                        dc = dq * 4 + c
                        ins = eng.transpose(psb6[:, hb * 512 + c * 128: hb * 512 + (c + 1) * 128],
                                            ei_tm[:, mt, dc * 128:(dc + 1) * 128], id_bf[:, :])
                    inc(ins, "t")
                onT(eit)
                td = CNT["t"]

                def eie(eng, e=e, mt=mt, dq=dq, hb=hb, td=td):
                    W(eng, "t", td)
                    for c in range(4):
                        dc = dq * 4 + c
                        ins = eng.activation(out=eib[e % 2][:, dc * 512 + mt * 128: dc * 512 + mt * 128 + 128],
                                             in_=psb6[:, hb * 512 + c * 128: hb * 512 + (c + 1) * 128],
                                             func=AF.Copy)
                    inc(ins, "s")
                onS(eie)
                psfree[6 + hb] = [("s", CNT["s"])]
        eiT_done[e] = CNT["t"]
        eiT_evac[e] = ("s", CNT["s"])

    if _PH >= 3:
        emit_wload(0)
        emit_wload(1)
        emit_gather(0)
        emit_eitrans(0)
        emit_gather(1)
        emit_eitrans(1)

    sct_chain = {0: 0, 1: 0}
    dv_free = [[] for _ in range(4)]
    silfree = []
    down_prev = 0

    for e in (range(E) if _PH >= 3 else []):
        b12 = wb12[(3 * e) % 2]
        b22 = wb12[(3 * e + 1) % 2]
        b33 = wb3[(3 * e + 2) % 2]
        gateFc = {}
        valFc = {}
        silFc = {}

        for fc in range(FC):
            pg = fc % 2

            def gate(eng, e=e, fc=fc, pg=pg, b12=b12, fr=list(psfree[pg])):
                if fc == 0:
                    wfree(eng, [wload_cnt[3 * e]])
                    wfree(eng, [eiT_evac[e]])
                    if e == 0:
                        W(eng, "s", ATT_S)
                        W(eng, "v", ATT_V)
                wfree(eng, fr)
                for dc in range(DC):
                    ins = eng.matmul(ps[pg][:, :],
                                     lhsT=b12[:, dc * DFF + fc * 128: dc * DFF + fc * 128 + 128],
                                     rhs=eib[e % 2][:, dc * 512:(dc + 1) * 512],
                                     start=(dc == 0), stop=(dc == DC - 1))
                inc(ins, "t")
            onT(gate)
            gateFc[fc] = CNT["t"]

            def gsilu(eng, fc=fc, pg=pg, fr=list(silfree)):
                W(eng, "t", gateFc[fc])
                wfree(eng, fr)
                inc(eng.activation(out=sil_all[:, fc * 512:(fc + 1) * 512], in_=ps[pg][:, :], func=AF.Silu), "s")
            onS(gsilu)
            silFc[fc] = CNT["s"]
            psfree[pg] = [("s", CNT["s"])]
        consumer_done[3 * e] = gateFc[FC - 1]
        if 3 * e + 2 < 3 * E:
            emit_wload(3 * e + 2)

        for fc in range(FC):
            pv = 2 + fc % 2

            def value(eng, e=e, fc=fc, pv=pv, b22=b22, fr=list(psfree[pv])):
                if fc == 0:
                    wfree(eng, [wload_cnt[3 * e + 1]])
                wfree(eng, fr)
                for dc in range(DC):
                    ins = eng.matmul(ps[pv][:, :],
                                     lhsT=b22[:, dc * DFF + fc * 128: dc * DFF + fc * 128 + 128],
                                     rhs=eib[e % 2][:, dc * 512:(dc + 1) * 512],
                                     start=(dc == 0), stop=(dc == DC - 1))
                inc(ins, "t")
            onT(value)
            valFc[fc] = CNT["t"]

            def hmul(eng, e=e, fc=fc, pv=pv, dp=down_prev):
                W(eng, "t", valFc[fc])
                W(eng, "s", silFc[fc])
                if fc == 0 and e > 0:
                    W(eng, "t", dp)
                inc(eng.tensor_tensor(out=h_fm[:, fc * 512:(fc + 1) * 512],
                                      in0=ps[pv][:, :], in1=sil_all[:, fc * 512:(fc + 1) * 512], op=OP.mult), "v")
            onV(hmul)
            psfree[pv] = [("v", CNT["v"])]
        consumer_done[3 * e + 1] = valFc[FC - 1]
        HD = CNT["v"]
        silfree = [("v", HD)]
        eifm_free[e + 2] = [("t", valFc[FC - 1])]

        if 3 * e + 3 < 3 * E:
            emit_wload(3 * e + 3)
        if e + 2 < E:
            emit_gather(e + 2)
            emit_eitrans(e + 2)

        downT = {}
        dvdone = {}
        for mt in range(4):
            bb, hh = divmod(mt, 2)
            for dhf in range(2):
                pd = 4 + (mt * 2 + dhf) % 2

                def down(eng, e=e, mt=mt, dhf=dhf, pd=pd, HD=HD, b33=b33, fr=list(psfree[pd])):
                    if mt == 0 and dhf == 0:
                        wfree(eng, [wload_cnt[3 * e + 2]])
                        W(eng, "v", HD)
                    wfree(eng, fr)
                    for fc in range(FC):
                        ins = eng.matmul(ps[pd][:, :],
                                         lhsT=h_fm[:, fc * 512 + mt * 128: fc * 512 + mt * 128 + 128],
                                         rhs=b33[:, fc * D + dhf * 512: fc * D + dhf * 512 + 512],
                                         start=(fc == 0), stop=(fc == FC - 1))
                    inc(ins, "t")
                onT(down)
                downT[(mt, dhf)] = CNT["t"]

                def devac(eng, e=e, mt=mt, dhf=dhf, pd=pd, bb=bb, hh=hh, fr=list(dv_free[mt])):
                    W(eng, "t", downT[(mt, dhf)])
                    if dhf == 0:
                        wfree(eng, fr)
                    be = bb * 32 + e
                    inc(eng.tensor_scalar(dv[mt][:, dhf * 512:(dhf + 1) * 512], ps[pd][:, :],
                                          ewT[:, hh * 64 + be: hh * 64 + be + 1], None, op0=OP.mult), "v")
                onV(devac)
                psfree[pd] = [("v", CNT["v"])]
                dvdone[mt] = CNT["v"]
        consumer_done[3 * e + 2] = downT[(3, 1)]
        down_prev = downT[(3, 1)]
        if 3 * e + 4 < 3 * E:
            emit_wload(3 * e + 4)

        for bb in range(2):
            for hh in range(2):
                mt = bb * 2 + hh

                def sct(eng, e=e, mt=mt, bb=bb, hh=hh, dvd=dvdone[mt], chain=sct_chain[bb]):
                    W(eng, "v", dvd)
                    if e == 0:
                        W(eng, "qzf", ZFILL_DONE)
                    W(eng, f"qc{bb}", chain)
                    be = bb * 32 + e
                    winc(eng.indirect_dma_start(
                        out=accf_d[:, :],
                        out_offset=bass.IndirectOffsetOnAxis(idxT_i[:, hh * 64 + be: hh * 64 + be + 1], 0),
                        in_=dv[mt][:, :], in_offset=None,
                        compute_op=OP.add), f"qc{bb}")
                if _PH >= 4:
                    onG(sct)
                sct_chain[bb] = CNT[f"qc{bb}"]
            dv_free[bb * 2] = [(f"qc{bb}", CNT[f"qc{bb}"])]
            dv_free[bb * 2 + 1] = [(f"qc{bb}", CNT[f"qc{bb}"])]

    SCT_ALL = [("qc0", CNT["qc0"]), ("qc1", CNT["qc1"])]
    MOE_S = CNT["s"]
    MOE_V = CNT["v"]

    accfree = {}
    for b in range(B):
        for t in range(NT):
            par = t % 2
            sp = par * 16
            prev = (b, t - 2) if t >= 2 else ((b - 1, NT + t - 2) if b > 0 else None)

            def fld(eng, b=b, t=t, par=par, prev=prev):
                if b == 0 and t == 0:
                    wfree(eng, SCT_ALL)
                if prev is not None:
                    wfree(eng, accfree[prev])
                wfree(eng, [x1st[(b, t)]])
                winc(eng.dma_start(out=acc_t[par][:, :], in_=acc_d[b, t * 128:(t + 1) * 128, :]), f"qx{par}")
                winc(eng.dma_start(out=x1f_t[par][:, :], in_=x1_d[b, t * 128:(t + 1) * 128, :]), f"qx{par}")
            onQ(fld)
            fq = CNT[f"qx{par}"]

            def fstat(eng, par=par, sp=sp, fq=fq, b=b, t=t):
                W(eng, f"qx{par}", fq)
                if b == 0 and t == 0:
                    W(eng, "s", MOE_S)
                    W(eng, "v", MOE_V)
                inc(ln_stats_v(eng, acc_t[par][:, :], sp, scrf[:, :]), "v")
            onV(fstat)
            vst = CNT["v"]

            def fsq(eng, sp=sp, vst=vst):
                W(eng, "v", vst)
                inc(eng.activation(out=stc[:, sp + 6:sp + 7], in_=stc[:, sp + 5:sp + 6],
                                   func=AF.Sqrt, bias=EPS), "s")
            onS(fsq)
            sst = CNT["s"]

            def frc(eng, sp=sp, sst=sst, par=par, fr=list(outtfree[par])):
                W(eng, "s", sst)
                wfree(eng, fr)
                vnext(eng, ln_rstd_v(eng, sp))
                vnext(eng, ln_apply_v(eng, out_t[par][:, :], acc_t[par][:, :], sp))
                inc(eng.tensor_add(out_t[par][:, :], out_t[par][:, :], x1f_t[par][:, :]), "v")
            onV(frc)
            accfree[(b, t)] = [("v", CNT["v"])]
            vfin = CNT["v"]

            def fst(eng, b=b, t=t, par=par, vfin=vfin):
                W(eng, "v", vfin)
                winc(eng.dma_start(out=y_d[b, t * 128:(t + 1) * 128, :], in_=out_t[par][:, :]), f"qf{par}")
            onQ(fst)
            outtfree[par] = [(f"qf{par}", CNT[f"qf{par}"])]

    FIN = [("qf0", CNT["qf0"]), ("qf1", CNT["qf1"])]

    def fin(eng, FIN=FIN):
        wfree(eng, FIN)
    onQ(fin)

    block_ctx.__exit__(None, None, None)
    for c in reversed(list(sem_ctx.values())):
        c.__exit__(None, None, None)
    for c in reversed(ps_ctx):
        c.__exit__(None, None, None)
    arena_ctx.__exit__(None, None, None)
    return nc


def kernel(L0_attn_x, L0_attn_W_qk, L0_attn_b_qk, L0_ffn_router_w1, L0_ffn_router_w2,
           L0_ffn_w1, L0_ffn_w2, L0_ffn_w3):
    x = np.ascontiguousarray(L0_attn_x, dtype=np.float32)
    Wqk = np.asarray(L0_attn_W_qk, dtype=np.float32)
    bqk_in = np.asarray(L0_attn_b_qk, dtype=np.float32)
    Wcat = np.concatenate([Wqk[0::2], Wqk[1::2]], axis=0)
    wqkT = np.ascontiguousarray(Wcat.T.reshape(DC, 128, 2048).transpose(1, 0, 2))
    bq = bqk_in[0::2].reshape(8, 128).T
    bk = bqk_in[1::2].reshape(8, 128).T
    bqk = np.ascontiguousarray(np.concatenate([bq, bk], axis=1))
    rw1T = np.ascontiguousarray(np.asarray(L0_ffn_router_w1, dtype=np.float32)
                                .reshape(DC, 128, RH).transpose(1, 0, 2))
    rw2 = np.ascontiguousarray(L0_ffn_router_w2, dtype=np.float32)
    w1h = np.ascontiguousarray(np.asarray(L0_ffn_w1, dtype=np.float32)
                               .reshape(E, DC, 128, DFF).transpose(0, 2, 1, 3))
    w2h = np.ascontiguousarray(np.asarray(L0_ffn_w2, dtype=np.float32)
                               .reshape(E, DC, 128, DFF).transpose(0, 2, 1, 3))
    w3h = np.ascontiguousarray(np.asarray(L0_ffn_w3, dtype=np.float32)
                               .reshape(E, FC, 128, D).transpose(0, 2, 1, 3))
    idbf = np.eye(128, dtype=ml_dtypes.bfloat16)
    idf32 = np.eye(128, dtype=np.float32)

    nc = build_nc()
    in_maps = []
    for c in range(8):
        in_maps.append({
            "x": np.ascontiguousarray(x[2 * c:2 * c + 2]),
            "wqkT": wqkT, "bqk": bqk, "rw1T": rw1T, "rw2": rw2,
            "w1h": w1h, "w2h": w2h, "w3h": w3h,
            "idbf": idbf, "idf32": idf32,
        })
    res = run_bass_kernel_spmd(nc, in_maps[:_NCORES], core_ids=list(range(_NCORES)))
    out = np.concatenate([np.asarray(r["y"]) for r in res.results], axis=0)
    if _NCORES < 8:
        out = np.concatenate([out] + [out[0:2]] * (8 - _NCORES), axis=0)
    return np.ascontiguousarray(out.reshape(16, S, D).astype(np.float32))


if __name__ == "__main__":
    nc = build_nc()
    print("build ok; instructions:", sum(1 for _ in nc.all_instructions()))


# version 13 (final)
# speedup vs baseline: 1.0290x; 1.0154x over previous
import numpy as np
import ml_dtypes
import concourse.bass as bass
import concourse.mybir as mybir
from concourse.bass_utils import run_bass_kernel_spmd
from concourse import library_config

F32 = mybir.dt.float32
BF16 = mybir.dt.bfloat16
I32 = mybir.dt.int32
U32 = mybir.dt.uint32
I16 = mybir.dt.int16
AF = mybir.ActivationFunctionType
OP = mybir.AluOpType
AXX = mybir.AxisListType.X

B = 2
S = 2048
D = 1024
H = 16
DH = 64
HP = 8
E = 16
DFF = 2048
CAP = 256
RH = 128
NT = S // 128
NI = S // 512
DC = D // 128
FC = DFF // 128

SD = DH ** -0.5
LAM = 1.0507009873554804934193349852946
ALPHA = 1.6732632423543772848170429916717
CA = LAM * S ** -0.5
CB = LAM * ALPHA * S ** -0.5
EPS = 1e-5
ECLAMP = 75.0
import math
EMIN = math.exp(-ECLAMP)
CBE = CB * math.exp(ECLAMP)
NSPLIT = 2

_DBG = False
_PH = 4
_NCORES = 8
_ATT = 4
_SEVAC = 1
_SIMTP = 1
_NHP = 8


def build_nc():
    nc = bass.Bass("TRN2", target_bir_lowering=False)
    dk = "ExternalOutput" if _DBG else "Internal"

    x_h = nc.dram_tensor("x", [B, S, D], F32, kind="ExternalInput")
    wqk_h = nc.dram_tensor("wqkT", [128, DC, 2048], F32, kind="ExternalInput")
    bqk_h = nc.dram_tensor("bqk", [128, 16], F32, kind="ExternalInput")
    rw1_h = nc.dram_tensor("rw1T", [128, DC, RH], F32, kind="ExternalInput")
    rw2_h = nc.dram_tensor("rw2", [RH, E], F32, kind="ExternalInput")
    w1_h = nc.dram_tensor("w1h", [E, 128, DC, DFF], F32, kind="ExternalInput")
    w2_h = nc.dram_tensor("w2h", [E, 128, DC, DFF], F32, kind="ExternalInput")
    w3_h = nc.dram_tensor("w3h", [E, 128, FC, D], F32, kind="ExternalInput")
    idb_h = nc.dram_tensor("idbf", [128, 128], BF16, kind="ExternalInput")
    idf_h = nc.dram_tensor("idf32", [128, 128], F32, kind="ExternalInput")
    y_h = nc.dram_tensor("y", [B, S, D], F32, kind="ExternalOutput")
    x1d_h = nc.dram_tensor("x1d", [B, S, D], F32, kind=dk)
    ln2d_h = nc.dram_tensor("ln2d", [B, S, D], BF16, kind=dk)
    accd_h = nc.dram_tensor("accd", [B, S, D], F32, kind=dk)

    x_d, y_d = x_h.ap(), y_h.ap()
    x1_d, ln2_d, acc_d = x1d_h.ap(), ln2d_h.ap(), accd_h.ap()
    ln2f_d = ln2d_h.ap().rearrange("b s d -> (b s) d")
    accf_d = accd_h.ap().rearrange("b s d -> (b s) d")

    KB = 1024
    placements = []

    def place(name, shape, dtype, off):
        h = nc.virtual_sbuf_tensor(name, shape, dtype)
        placements.append((h, off))
        return h

    OA = 0
    wqk = place("wqk", [128, DC * 2048], BF16, OA)
    wb12_0 = place("wb12_0", [128, DC * DFF], BF16, OA)
    wb3_0 = place("wb3_0", [128, FC * D], BF16, OA)
    OB = 32 * KB
    k_fm = place("k_fm", [128, DC * 2048], BF16, OB)
    wb12_1 = place("wb12_1", [128, DC * DFF], BF16, OB)
    wb3_1 = place("wb3_1", [128, FC * D], BF16, OB)
    OC = 64 * KB
    q_fm = place("q_fm", [128, DC * 2048], BF16, OC)
    ei0 = place("ei0", [128, DC * 512], BF16, OC)
    ei1 = place("ei1", [128, DC * 512], BF16, OC + 8 * KB)
    h_fm = place("h_fm", [128, FC * 512], BF16, OC + 16 * KB)
    OD = 96 * KB
    ln1_tm = place("ln1_tm", [128, NT * D], BF16, OD)
    dv = [place(f"dv{i}", [128, D], F32, OD + i * 4 * KB) for i in range(4)]
    acc_t = [place(f"acct{i}", [128, D], F32, OD + 16 * KB + i * 4 * KB) for i in range(2)]
    x1f_t = [place(f"x1ft{i}", [128, D], F32, OD + 24 * KB + i * 4 * KB) for i in range(2)]
    OE = 128 * KB
    slab = place("slab", [128, DC * 512], BF16, OE)
    scrb = place("scrb", [128, D], BF16, OE)
    scrf = place("scrf", [128, D], F32, OE)
    ln_t = place("ln_t", [128, D], F32, OE + 4 * KB)
    sil_all = place("sil_all", [128, FC * 512], BF16, OE)
    out_t = [place("outt0", [128, D], F32, OE + 4 * KB),
             place("outt1", [128, D], F32, 136 * KB)]
    OF = 136 * KB
    out_fm = place("out_fm", [128, HP * 512], BF16, OF)
    ei_tm = place("ei_tm", [128, 4, D], BF16, 144 * KB)
    OG = 144 * KB
    eb = [place(f"eb{i}", [128, 2 * 512], BF16, OG + i * 2 * KB) for i in range(2)]
    apb = [place(f"apb{i}", [128, 2 * 512], BF16, OG + 4 * KB + i * 2 * KB) for i in range(2)]
    bpb = [place(f"bpb{i}", [128, 2 * 512], BF16, OG + 8 * KB + i * 2 * KB) for i in range(2)]
    OX = 156 * KB
    x_t = [place(f"xt{i}", [128, D], F32, OX + i * 4 * KB) for i in range(2)]
    OX1 = 164 * KB
    x1_t = [place(f"x1t{i}", [128, D], F32, OX1 + i * 4 * KB) for i in range(2)]
    OL = 172 * KB
    logits = place("logits", [64, S], F32, OL)
    zero_sb = place("zero_sb", [128, D], F32, OF)

    off = [180 * KB]

    def salloc(name, shape, dtype):
        per = int(np.prod(shape[1:])) * mybir.dt.size(dtype)
        o = (off[0] + 31) // 32 * 32
        h = place(name, shape, dtype, o)
        off[0] = o + per
        return h, o

    id_bf, _ = salloc("id_bf", [128, 128], BF16)
    id_f, _ = salloc("id_f", [128, 128], F32)
    bqk, _ = salloc("bqk_sb", [128, 16], F32)
    rw2, _ = salloc("rw2_sb", [RH, E], BF16)
    rw1, _ = salloc("rw1_sb", [128, DC * RH], BF16)
    stc, _ = salloc("stc", [128, 32], F32)
    kbar, _ = salloc("kbar", [128, DC], F32)
    colsv, _ = salloc("colsv", [128, DC], F32)
    corr, _ = salloc("corr", [128, DC], F32)
    tmpc, _ = salloc("tmpc", [128, 2], F32)
    r1bf, _ = salloc("r1bf", [128, 128], BF16)
    st2, _ = salloc("st2", [64, 8], F32)
    vals, OVALS = salloc("vals", [64, CAP], F32)
    idx, _ = salloc("idx", [64, CAP], U32)
    idxf, OIDXF = salloc("idxf", [64, CAP], F32)
    idxT_i, _ = salloc("idxT_i", [128, 128], U32)
    idxT_h, _ = salloc("idxT_h", [128, 128], I16)
    ewT, _ = salloc("ewT", [128, 128], F32)
    idxg, OIDXG = salloc("idxg", [128, 1024], I16)
    cst, _ = salloc("cst", [128, 8], F32)
    out_tm = place("out_tm", [128, D], BF16, OVALS)
    ln2_t = place("ln2_t", [128, D], BF16, OIDXF)
    ln2f_t = place("ln2f_t", [128, D], BF16, OIDXG)
    assert off[0] <= 192 * KB, off[0]

    arena_ctx = nc.sbuf_tensor("arena", [128, 192 * KB], mybir.dt.uint8)
    arena_h = arena_ctx.__enter__()
    base = nc.lookup_mloc(arena_h).addr
    for h, o in placements:
        nc._tensor(h.name, h.shape, h.dtype, type="SB")
        ml = nc.lookup_mloc(h)
        ml.allocated = True
        ml.addr = base + o

    ps_ctx = [nc.psum_tensor(f"ps{i}", [128, 512], F32) for i in range(6)]
    ps = [c.__enter__() for c in ps_ctx]
    ps6_ctx = nc.psum_tensor("ps6b", [128, 1024], BF16)
    psb6 = ps6_ctx.__enter__()
    ps_ctx.append(ps6_ctx)
    ps7_ctx = nc.psum_tensor("ps7", [128, 512], F32)
    ps7 = ps7_ctx.__enter__()
    ps_ctx.append(ps7_ctx)
    ps = ps + [None, ps7]

    sem_ctx = {}
    sems = {}
    for nme in ["qcst", "qx0", "qx1", "qzf", "qs0", "qs1", "qlns", "qidx", "qwp",
                "qw0", "qw1", "qw2", "qg0", "qg1", "qc0", "qc1", "qf0", "qf1",
                "t", "v", "s", "vv"]:
        c = nc.semaphore(f"sem_{nme}")
        sem_ctx[nme] = c
        sems[nme] = c.__enter__()
    CNT = {k: 0 for k in sems}

    block_ctx = nc.Block()
    block = block_ctx.__enter__()

    def inc(instr, nme, k=1):
        instr.then_inc(sems[nme], k)
        CNT[nme] += k
        return CNT[nme]

    def winc(instr, nme):
        return inc(instr, nme, 16)

    def W(eng, nme, val):
        if val > 0:
            eng.wait_ge(sems[nme], val)

    def wfree(eng, conds):
        if isinstance(conds, tuple):
            conds = [conds]
        for nme, v in conds:
            W(eng, nme, v)

    def onT(f):
        block.tensor(f)

    def onV(f):
        block.vector(f)

    def onS(f):
        block.scalar(f)

    def onQ(f):
        block.sync(f)

    def onG(f):
        block.gpsimd(f)

    def vnext(eng, instr):
        inc(instr, "vv")
        W(eng, "vv", CNT["vv"])

    def ln_stats_v(eng, src, sp, dump):
        eng.reduce_sum(stc[:, sp:sp + 1], src, axis=AXX)
        vnext(eng, eng.scalar_tensor_tensor(out=dump, in0=src, scalar=1.0, in1=src,
                                            op0=OP.mult, op1=OP.mult, accum_out=stc[:, sp + 1:sp + 2]))
        eng.tensor_scalar(stc[:, sp + 2:sp + 3], stc[:, sp:sp + 1], 1.0 / D, None, op0=OP.mult)
        vnext(eng, eng.tensor_scalar(stc[:, sp + 3:sp + 4], stc[:, sp + 1:sp + 2], 1.0 / D, None, op0=OP.mult))
        vnext(eng, eng.tensor_tensor(stc[:, sp + 4:sp + 5], stc[:, sp + 2:sp + 3], stc[:, sp + 2:sp + 3], op=OP.mult))
        return eng.tensor_sub(stc[:, sp + 5:sp + 6], stc[:, sp + 3:sp + 4], stc[:, sp + 4:sp + 5])

    def ln_rstd_v(eng, sp):
        return eng.reciprocal(stc[:, sp + 7:sp + 8], stc[:, sp + 6:sp + 7])

    def ln_apply_v(eng, out_ap, in_ap, sp):
        return eng.tensor_scalar(out_ap, in_ap, stc[:, sp + 2:sp + 3], stc[:, sp + 7:sp + 8],
                                 op0=OP.subtract, op1=OP.mult)

    def prelude_q(eng):
        winc(eng.dma_start(out=id_bf[:, :], in_=idb_h.ap()[:, :]), "qcst")
        winc(eng.dma_start(out=id_f[:, :], in_=idf_h.ap()[:, :]), "qcst")
        winc(eng.dma_start(out=bqk[:, :], in_=bqk_h.ap()[:, :]), "qcst")
    onQ(prelude_q)
    CONSTS_Q = CNT["qcst"]

    def prelude_g(eng):
        winc(eng.dma_start(out=wqk[:, :], in_=wqk_h.ap()[:, :, :]), "qwp")
        winc(eng.dma_start(out=rw1[:, :], in_=rw1_h.ap()[:, :, :]), "qwp")
        winc(eng.dma_start(out=rw2[:, :], in_=rw2_h.ap()[:, :]), "qwp")
    onG(prelude_g)
    WQK_LOADED = CNT["qwp"]

    CONSTS = [EPS, CA, -ECLAMP]
    for ci, cv in enumerate(CONSTS):
        nc.const_aps.register(F32, cv, cst[:, ci:ci + 1])

    def prelude_v(eng):
        eng.memset(zero_sb[:, :], 0.0)
        eng.memset(logits[:, :], 0.0)
        for ci, cv in enumerate(CONSTS):
            eng.memset(cst[:, ci:ci + 1], cv)
        vnext(eng, eng.memset(colsv[:, :], 0.0))
        inc(eng.memset(tmpc[:, 1:2], 0.0), "v")
    onV(prelude_v)
    ZMEM = CNT["v"]

    def zfill(eng):
        W(eng, "v", ZMEM)
        for b in range(B):
            for t in range(NT):
                winc(eng.dma_start(out=acc_d[b, t * 128:(t + 1) * 128, :], in_=zero_sb[:, :]), "qzf")
    onQ(zfill)
    ZFILL_DONE = CNT["qzf"]

    xfree = [[], []]
    x1free = [[], []]
    ln2free = []
    out_tm_free = []
    ln2f_free = []
    outtfree = [[], []]
    psfree = [[("v", 0)]] * 8
    psfree = [[] for _ in range(8)]
    ebfree = [[], []]
    apfree = [[], []]
    LNST = {}
    lnst = [0, 0]
    x1st = {}
    G = {"proj_i": 0, "a6T": 0, "a6V": 0, "slabsum": 0}

    def attention(b):
        nonlocal ln2free, out_tm_free, ln2f_free
        for i in range(NI):
            appl = [0] * 4
            for tc in range(4):
                t = i * 4 + tc
                par = t % 2
                sp = par * 16

                def ld(eng, b=b, t=t, par=par, fr=list(xfree[par])):
                    wfree(eng, fr)
                    winc(eng.dma_start(out=x_t[par][:, :], in_=x_d[b, t * 128:(t + 1) * 128, :]), f"qx{par}")
                onQ(ld)
                xq = CNT[f"qx{par}"]

                def stats(eng, par=par, sp=sp, xq=xq, fr=list(x1free[0]), b=b, t=t):
                    W(eng, f"qx{par}", xq)
                    if b == 0 and t == 0:
                        W(eng, "qcst", CONSTS_Q)
                    wfree(eng, fr)
                    inc(ln_stats_v(eng, x_t[par][:, :], sp, x1_t[0][:, :]), "v")
                onV(stats)
                vst = CNT["v"]

                def rstd(eng, sp=sp, vst=vst):
                    W(eng, "v", vst)
                    inc(eng.activation(out=stc[:, sp + 6:sp + 7], in_=stc[:, sp + 5:sp + 6],
                                       func=AF.Sqrt, bias=EPS), "s")
                onS(rstd)
                sst = CNT["s"]

                def rcpb(eng, sp=sp, sst=sst, par=par, t=t):
                    W(eng, "s", sst)
                    vnext(eng, ln_rstd_v(eng, sp))
                    inc(ln_apply_v(eng, ln1_tm[:, t * D:(t + 1) * D], x_t[par][:, :], sp), "v")
                onV(rcpb)
                appl[tc] = ("v", CNT["v"])
                xfree[par] = [("v", CNT["v"])]

            for tc in range(4):
                t = i * 4 + tc
                for dq in range(2):
                    hb = dq % 2

                    def tpose(eng, t=t, dq=dq, hb=hb, ap4=appl[tc], fr=list(psfree[6 + hb])):
                        wfree(eng, [ap4])
                        wfree(eng, fr)
                        for c in range(4):
                            dc = dq * 4 + c
                            ins = eng.transpose(psb6[:, hb * 512 + c * 128: hb * 512 + (c + 1) * 128],
                                                ln1_tm[:, t * D + dc * 128: t * D + dc * 128 + 128],
                                                id_bf[:, :])
                        inc(ins, "t")
                    onT(tpose)
                    tdone = CNT["t"]

                    def tevac(eng, tc=tc, dq=dq, hb=hb, tdone=tdone,
                              pj=G["proj_i"], ss=G["slabsum"], av=G["a6V"]):
                        W(eng, "t", tdone)
                        W(eng, "t", pj)
                        W(eng, "v", ss)
                        W(eng, "v", av)
                        for c in range(4):
                            dc = dq * 4 + c
                            ins = eng.activation(out=slab[:, dc * 512 + tc * 128: dc * 512 + tc * 128 + 128],
                                                 in_=psb6[:, hb * 512 + c * 128: hb * 512 + (c + 1) * 128], func=AF.Copy)
                        inc(ins, "s")
                    onS(tevac)
                    psfree[6 + hb] = [("s", CNT["s"])]
            slab_ready = ("s", CNT["s"])

            def slabsum(eng, sr=slab_ready):
                wfree(eng, [sr])
                for dc in range(DC):
                    vnext(eng, eng.reduce_sum(tmpc[:, 0:1], slab[:, dc * 512:(dc + 1) * 512], axis=AXX))
                    ins = eng.tensor_add(colsv[:, dc:dc + 1], colsv[:, dc:dc + 1], tmpc[:, 0:1])
                    if dc < DC - 1:
                        vnext(eng, ins)
                inc(ins, "v")
            onV(slabsum)
            G["slabsum"] = CNT["v"]

            for fc in (range(16) if _ATT >= 2 else []):
                bank = (i * 16 + fc) % 2

                def proj(eng, fc=fc, bank=bank, sr=slab_ready, i=i, fr=list(psfree[bank])):
                    wfree(eng, [sr])
                    if i == 0 and fc == 0:
                        W(eng, "qwp", WQK_LOADED)
                    wfree(eng, fr)
                    for dc in range(DC):
                        ins = eng.matmul(ps[bank][:, :],
                                         lhsT=wqk[:, dc * 2048 + fc * 128: dc * 2048 + fc * 128 + 128],
                                         rhs=slab[:, dc * 512:(dc + 1) * 512],
                                         start=(dc == 0), stop=(dc == DC - 1))
                    inc(ins, "t")
                onT(proj)
                pj = CNT["t"]

                def pevac(eng, fc=fc, bank=bank, i=i, pj=pj):
                    W(eng, "t", pj)
                    dst = q_fm if fc < 8 else k_fm
                    c = fc if fc < 8 else fc - 8
                    inc(eng.tensor_scalar(dst[:, c * 2048 + i * 512: c * 2048 + i * 512 + 512],
                                          ps[bank][:, :], bqk[:, fc:fc + 1], None, op0=OP.add), "v")
                onV(pevac)
                psfree[bank] = [("v", CNT["v"])]
            G["proj_i"] = CNT["t"]
        PROJ_EVAC = CNT["v"]

        def kcenter(eng, pe=PROJ_EVAC):
            W(eng, "v", pe)
            W(eng, "vv", CNT["vv"])
            for dc in range(DC):
                vnext(eng, eng.reduce_sum(kbar[:, dc:dc + 1], k_fm[:, dc * 2048:(dc + 1) * 2048], axis=AXX))
                vnext(eng, eng.tensor_scalar(kbar[:, dc:dc + 1], kbar[:, dc:dc + 1], 1.0 / S, None, op0=OP.mult))
                eng.tensor_scalar(k_fm[:, dc * 2048:(dc + 1) * 2048], k_fm[:, dc * 2048:(dc + 1) * 2048],
                                  kbar[:, dc:dc + 1], SD, op0=OP.subtract, op1=OP.mult)
            vnext(eng, eng.tensor_scalar(corr[:, :], colsv[:, :], CB, None, op0=OP.mult))
            inc(eng.memset(colsv[:, :], 0.0), "v")
        if _ATT >= 2:
            onV(kcenter)
        KC = CNT["v"]

        g = 0
        simdone = {}
        expdone = {}
        aprdone = {}
        av_last = 0
        for i in range(NI):
            for hp in range(HP if _ATT >= 4 else _NHP):
                for j in (range(NT + 1) if _ATT >= 3 else []):
                    if j < NT:
                        pA, pB = g % 2, 2 + g % 2

                        def sim(eng, i=i, hp=hp, j=j, g=g, pA=pA, pB=pB, kc=KC,
                                frA=list(psfree[pA]), frB=list(psfree[pB])):
                            if g == 0:
                                W(eng, "v", kc)
                            if g >= 2:
                                W(eng, "s", expdone[g - 2])
                                W(eng, "v", aprdone[g - 2])
                            else:
                                wfree(eng, frA)
                                wfree(eng, frB)
                            if _SIMTP:
                                eng.matmul(ps[pA][:, :],
                                           lhsT=k_fm[0:64, hp * 2048 + j * 128: hp * 2048 + j * 128 + 128],
                                           rhs=q_fm[0:64, hp * 2048 + i * 512: hp * 2048 + i * 512 + 512],
                                           start=True, stop=True, tile_position=(0, 0))
                                inc(eng.matmul(ps[pB][:, :],
                                               lhsT=k_fm[64:128, hp * 2048 + j * 128: hp * 2048 + j * 128 + 128],
                                               rhs=q_fm[64:128, hp * 2048 + i * 512: hp * 2048 + i * 512 + 512],
                                               start=True, stop=True, tile_position=(64, 0)), "t")
                            else:
                                eng.matmul(ps[pA][:, :],
                                           lhsT=k_fm[:, hp * 2048 + j * 128: hp * 2048 + j * 128 + 128],
                                           rhs=q_fm[:, hp * 2048 + i * 512: hp * 2048 + i * 512 + 512],
                                           start=True, stop=True)
                                inc(eng.matmul(ps[pB][:, :],
                                               lhsT=k_fm[:, hp * 2048 + j * 128: hp * 2048 + j * 128 + 128],
                                               rhs=q_fm[:, hp * 2048 + i * 512: hp * 2048 + i * 512 + 512],
                                               start=True, stop=True), "t")
                        onT(sim)
                        simdone[g] = CNT["t"]
                        s_does_a = (j % 8) < NSPLIT
                        ep = g % 2

                        def sevac(eng, g=g, pA=pA, pB=pB, ep=ep, s_does_a=s_does_a,
                                  fre=list(ebfree[ep]), fra=list(apfree[ep])):
                            W(eng, "t", simdone[g])
                            wfree(eng, fre)
                            eng.activation(out=eb[ep][:, 0:512], in_=ps[pA][:, :], func=AF.Exp, bias=-ECLAMP)
                            ins = eng.activation(out=eb[ep][:, 512:1024], in_=ps[pB][:, :], func=AF.Exp, bias=-ECLAMP)
                            if s_does_a:
                                wfree(eng, fra)
                                if _SEVAC:
                                    eng.activation(out=apb[ep][:, 0:512], in_=ps[pA][:, :], func=AF.Relu, scale=CA)
                                    ins = eng.activation(out=apb[ep][:, 512:1024], in_=ps[pB][:, :], func=AF.Relu, scale=CA)
                                else:
                                    eng.activation(out=apb[ep][:, 0:512], in_=ps[pA][:, :], func=AF.Copy)
                                    ins = eng.activation(out=apb[ep][:, 512:1024], in_=ps[pB][:, :], func=AF.Copy)
                            inc(ins, "s")
                        onS(sevac)
                        expdone[g] = CNT["s"]

                        def vevac(eng, g=g, pA=pA, pB=pB, ep=ep, s_does_a=s_does_a,
                                  fra=list(apfree[ep])):
                            if not s_does_a:
                                W(eng, "t", simdone[g])
                                W(eng, "s", expdone[g])
                                wfree(eng, fra)
                                if _SEVAC:
                                    eng.tensor_scalar(apb[ep][:, 0:512], ps[pA][:, :], 0.0, CA, op0=OP.max, op1=OP.mult)
                                    eng.tensor_scalar(apb[ep][:, 512:1024], ps[pB][:, :], 0.0, CA, op0=OP.max, op1=OP.mult)
                                else:
                                    eng.tensor_copy(apb[ep][:, 0:512], ps[pA][:, :])
                                    eng.tensor_copy(apb[ep][:, 512:1024], ps[pB][:, :])
                            W(eng, "s", expdone[g])
                            eng.tensor_scalar(bpb[ep][:, 0:512], eb[ep][:, 0:512], EMIN, CBE, op0=OP.min, op1=OP.mult)
                            inc(eng.tensor_scalar(bpb[ep][:, 512:1024], eb[ep][:, 512:1024], EMIN, CBE,
                                                  op0=OP.min, op1=OP.mult), "v")
                        onV(vevac)
                        aprdone[g] = CNT["v"]
                        ebfree[ep] = [("v", CNT["v"])]
                        psfree[pA] = [("s", expdone[g]), ("v", aprdone[g])]
                        psfree[pB] = [("s", expdone[g]), ("v", aprdone[g])]
                    if j > 0 and _ATT >= 4:
                        ja = j - 1
                        ga = g - 1
                        ep = ga % 2

                        def attnv(eng, hp=hp, ja=ja, ga=ga, ep=ep,
                                  fr4=list(psfree[4]), fr5=list(psfree[5])):
                            W(eng, "v", aprdone[ga])
                            W(eng, "s", expdone[ga])
                            if ja == 0:
                                wfree(eng, fr4)
                                wfree(eng, fr5)
                            vA = ln1_tm[:, ja * D + hp * 128: ja * D + hp * 128 + 64]
                            vB = ln1_tm[:, ja * D + hp * 128 + 64: ja * D + hp * 128 + 128]
                            st = (ja == 0)
                            sp_ = (ja == NT - 1)
                            eng.matmul(ps[4][0:64, :], lhsT=vA, rhs=apb[ep][:, 0:512],
                                       start=st, stop=False, tile_position=(0, 0))
                            eng.matmul(ps[4][0:64, :], lhsT=vA, rhs=bpb[ep][:, 0:512],
                                       start=False, stop=sp_, tile_position=(0, 0))
                            eng.matmul(ps[5][64:128, :], lhsT=vB, rhs=apb[ep][:, 512:1024],
                                       start=st, stop=False, tile_position=(0, 64))
                            inc(eng.matmul(ps[5][64:128, :], lhsT=vB, rhs=bpb[ep][:, 512:1024],
                                           start=False, stop=sp_, tile_position=(0, 64)), "t")
                        onT(attnv)
                        av_last = CNT["t"]
                        apfree[ep] = [("t", CNT["t"])]
                    if j < NT:
                        g += 1

                def oevac(eng, hp=hp, i=i, av=av_last, a6T=G["a6T"], b=b):
                    W(eng, "t", av)
                    W(eng, "t", a6T)
                    if b == 0 and i == 0:
                        W(eng, "qzf", ZFILL_DONE)
                    eng.tensor_scalar(out_fm[0:64, hp * 512:(hp + 1) * 512], ps[4][0:64, :],
                                      corr[0:64, hp:hp + 1], None, op0=OP.subtract)
                    inc(eng.tensor_scalar(out_fm[64:128, hp * 512:(hp + 1) * 512], ps[5][64:128, :],
                                          corr[64:128, hp:hp + 1], None, op0=OP.subtract), "v")
                if _ATT >= 4:
                    onV(oevac)
                    psfree[4] = [("v", CNT["v"])]
                    psfree[5] = [("v", CNT["v"])]
            OUTE = CNT["v"]

            for tc in range(4):
                t = i * 4 + tc
                par = t % 2
                sp = par * 16

                def ldx(eng, b=b, t=t, par=par, fr=list(xfree[par])):
                    wfree(eng, fr)
                    winc(eng.dma_start(out=x_t[par][:, :], in_=x_d[b, t * 128:(t + 1) * 128, :]), f"qx{par}")
                onQ(ldx)
                xq = CNT[f"qx{par}"]

                ev = 0
                for dq in range(2):
                    hb = dq % 2

                    def a6t(eng, tc=tc, dq=dq, hb=hb, oe=OUTE, fr=list(psfree[6 + hb])):
                        W(eng, "v", oe)
                        wfree(eng, fr)
                        for c in range(4):
                            dc = dq * 4 + c
                            ins = eng.transpose(psb6[:, hb * 512 + c * 128: hb * 512 + (c + 1) * 128],
                                                out_fm[:, dc * 512 + tc * 128: dc * 512 + tc * 128 + 128],
                                                id_bf[:, :])
                        inc(ins, "t")
                    onT(a6t)
                    td = CNT["t"]

                    def a6e(eng, dq=dq, hb=hb, td=td, fr=list(out_tm_free)):
                        W(eng, "t", td)
                        if dq == 0:
                            wfree(eng, fr)
                        inc(eng.activation(out=out_tm[:, dq * 512:(dq + 1) * 512],
                                           in_=psb6[:, hb * 512: hb * 512 + 512], func=AF.Copy), "s")
                    onS(a6e)
                    psfree[6 + hb] = [("s", CNT["s"])]
                    ev = CNT["s"]

                def a6v(eng, sp=sp, ev=ev):
                    W(eng, "s", ev)
                    inc(ln_stats_v(eng, out_tm[:, :], sp, scrb[:, :]), "v")
                onV(a6v)
                out_tm_free = [("v", CNT["v"])]
                vst = CNT["v"]

                def a6s(eng, sp=sp, vst=vst):
                    W(eng, "v", vst)
                    inc(eng.activation(out=stc[:, sp + 6:sp + 7], in_=stc[:, sp + 5:sp + 6],
                                       func=AF.Sqrt, bias=EPS), "s")
                onS(a6s)
                sst = CNT["s"]

                def a6r(eng, sp=sp, sst=sst):
                    W(eng, "s", sst)
                    vnext(eng, ln_rstd_v(eng, sp))
                    inc(ln_apply_v(eng, ln_t[:, :], out_tm[:, :], sp), "v")
                onV(a6r)
                out_tm_free.append(("v", CNT["v"]))
                sap = CNT["v"]

                def a6add(eng, par=par, sp=sp, xq=xq, sap=sap, fr=list(x1free[par])):
                    W(eng, f"qx{par}", xq)
                    W(eng, "v", sap)
                    wfree(eng, fr)
                    vnext(eng, eng.tensor_add(x1_t[par][:, :], ln_t[:, :], x_t[par][:, :]))
                    inc(ln_stats_v(eng, x1_t[par][:, :], sp, scrf[:, :]), "v")
                onV(a6add)
                xfree[par] = [("v", CNT["v"])]
                vadd = CNT["v"]

                def x1store(eng, b=b, t=t, par=par, vadd=vadd):
                    W(eng, "v", vadd)
                    winc(eng.dma_start(out=x1_d[b, t * 128:(t + 1) * 128, :], in_=x1_t[par][:, :]), f"qs{par}")
                onQ(x1store)
                x1st[(b, t)] = (f"qs{par}", CNT[f"qs{par}"])

                def l2s(eng, sp=sp, vadd=vadd):
                    W(eng, "v", vadd)
                    inc(eng.activation(out=stc[:, sp + 6:sp + 7], in_=stc[:, sp + 5:sp + 6],
                                       func=AF.Sqrt, bias=EPS), "s")
                onS(l2s)
                sst2 = CNT["s"]

                def l2r(eng, sp=sp, sst2=sst2, par=par, fr=list(ln2free)):
                    W(eng, "s", sst2)
                    wfree(eng, fr)
                    vnext(eng, ln_rstd_v(eng, sp))
                    inc(ln_apply_v(eng, ln2_t[:, :], x1_t[par][:, :], sp), "v")
                onV(l2r)
                x1free[par] = [("v", CNT["v"]), x1st[(b, t)]]
                sl2 = CNT["v"]

                def l2store(eng, b=b, t=t, sl2=sl2):
                    W(eng, "v", sl2)
                    winc(eng.dma_start(out=ln2_d[b, t * 128:(t + 1) * 128, :], in_=ln2_t[:, :]), "qlns")
                onQ(l2store)
                lnst[b] = CNT["qlns"]
                ln2free = [("qlns", CNT["qlns"])]

                for dq in range(2):
                    hb = dq % 2

                    def l2t(eng, dq=dq, hb=hb, sl2=sl2, fr=list(psfree[6 + hb])):
                        W(eng, "v", sl2)
                        wfree(eng, fr)
                        for c in range(4):
                            dc = dq * 4 + c
                            ins = eng.transpose(psb6[:, hb * 512 + c * 128: hb * 512 + (c + 1) * 128],
                                                ln2_t[:, dc * 128:(dc + 1) * 128], id_bf[:, :])
                        inc(ins, "t")
                    onT(l2t)
                    td = CNT["t"]

                    def l2te(eng, dq=dq, hb=hb, td=td, fr=list(ln2f_free)):
                        W(eng, "t", td)
                        if dq == 0:
                            wfree(eng, fr)
                        for c in range(4):
                            dc = dq * 4 + c
                            ins = eng.activation(out=ln2f_t[:, dc * 128:(dc + 1) * 128],
                                                 in_=psb6[:, hb * 512 + c * 128: hb * 512 + (c + 1) * 128], func=AF.Copy)
                        inc(ins, "s")
                    onS(l2te)
                    psfree[6 + hb] = [("s", CNT["s"])]
                ln2free.append(("t", 0))
                L2F = ("s", CNT["s"])

                prevq = (b, t - 4) if t >= 4 else ((0, t + 12) if b == 1 else None)

                def router(eng, L2F=L2F, t=t, prevq=prevq):
                    wfree(eng, [L2F])
                    if prevq is not None:
                        W(eng, "v", LNST[prevq])
                    tq = t % 4
                    for dc in range(DC):
                        ins = eng.matmul(ps7[:, tq * 128:(tq + 1) * 128], lhsT=rw1[:, dc * RH:(dc + 1) * RH],
                                         rhs=ln2f_t[:, dc * 128:(dc + 1) * 128],
                                         start=(dc == 0), stop=(dc == DC - 1))
                    inc(ins, "t")
                onT(router)
                rt = CNT["t"]
                ln2free[-1] = ("t", rt)
                ln2f_free = [("t", rt)]

                def rsilu(eng, rt=rt, t=t):
                    W(eng, "t", rt)
                    inc(eng.activation(out=r1bf[:, :], in_=ps7[:, (t % 4) * 128:(t % 4) * 128 + 128], func=AF.Silu), "s")
                onS(rsilu)
                rs = CNT["s"]

                def rlog(eng, rs=rs, b=b, t=t):
                    W(eng, "s", rs)
                    tp = (0, 0) if b == 0 else (0, 32)
                    r0 = 0 if b == 0 else 32
                    inc(eng.matmul(ps7[r0:r0 + 16, (t % 4) * 128:(t % 4) * 128 + 128],
                                   lhsT=rw2[:, :], rhs=r1bf[:, :], start=True, stop=True,
                                   tile_position=tp), "t")
                onT(rlog)
                lg = CNT["t"]

                def rlev(eng, lg=lg, b=b, t=t):
                    W(eng, "t", lg)
                    r0 = 0 if b == 0 else 32
                    inc(eng.tensor_copy(out=logits[b * 32:b * 32 + 16, t * 128:(t + 1) * 128],
                                        in_=ps7[r0:r0 + 16, (t % 4) * 128:(t % 4) * 128 + 128]), "v")
                onV(rlev)
                LNST[(b, t)] = CNT["v"]
            G["a6T"] = CNT["t"]
            G["a6V"] = CNT["v"]
        return CNT["t"]

    attention(0)
    PROJ_B1_MARK = None
    attention(1)
    PROJ_B1 = G["proj_i"]
    SIM_B1 = CNT["t"]
    ATT_S = CNT["s"]
    ATT_V = CNT["v"]

    tkv = CNT["v"]

    def topk1(eng):
        W(eng, "v", ATT_V)
        vnext(eng, eng.reduce_max(st2[:, 0:1], logits[:, :], axis=AXX))
        inc(eng.tensor_scalar(st2[:, 1:2], st2[:, 0:1], -1.0, None, op0=OP.mult), "v")
    if _PH >= 2:
        onV(topk1)
    tk1 = CNT["v"]

    def texp(eng, tk1=tk1):
        W(eng, "v", tk1)
        inc(eng.activation(out=logits[:, :], in_=logits[:, :], func=AF.Exp, bias=st2[:, 1:2]), "s")
    if _PH >= 2:
        onS(texp)
    tke = CNT["s"]

    def trounds(eng, tke=tke, l1=lnst[1]):
        W(eng, "s", tke)
        W(eng, "qlns", l1)
        eng.memset(idxg[:, :], 0)
        vnext(eng, eng.reduce_sum(st2[:, 2:3], logits[:, :], axis=AXX))
        vnext(eng, eng.reciprocal(st2[:, 3:4], st2[:, 2:3]))
        for r in range(32):
            vnext(eng, eng.max(out=vals[:, r * 8:(r + 1) * 8], in_=logits[:, :]))
            vnext(eng, eng.max_index(out=idx[:, r * 8:(r + 1) * 8], in_max=vals[:, r * 8:(r + 1) * 8],
                                     in_values=logits[:, :]))
            vnext(eng, eng.match_replace(out=logits[:, :], in_to_replace=vals[:, r * 8:(r + 1) * 8],
                                         in_values=logits[:, :], imm_value=-1.0))
        vnext(eng, eng.tensor_scalar(vals[:, :], vals[:, :], st2[:, 3:4], None, op0=OP.mult))
        vnext(eng, eng.reduce_sum(st2[:, 4:5], vals[:, :], axis=AXX))
        vnext(eng, eng.tensor_scalar(st2[:, 4:5], st2[:, 4:5], 1e-9, None, op0=OP.add))
        vnext(eng, eng.reciprocal(st2[:, 5:6], st2[:, 4:5]))
        vnext(eng, eng.tensor_scalar(vals[:, :], vals[:, :], st2[:, 5:6], None, op0=OP.mult))
        vnext(eng, eng.tensor_copy(out=idxf[:, :], in_=idx[:, :]))
        inc(eng.tensor_scalar(idxf[32:64, :], idxf[32:64, :], 2048.0, None, op0=OP.add), "v")
    if _PH >= 2:
        onV(trounds)
    tkr = CNT["v"]

    def ttrans(eng, tkr=tkr):
        W(eng, "v", tkr)
        eng.transpose(ps7[:, 0:64], idxf[0:64, 0:128], id_f[0:64, 0:64])
        eng.transpose(ps7[:, 64:128], idxf[0:64, 128:256], id_f[0:64, 0:64])
        eng.transpose(ps7[:, 128:192], vals[0:64, 0:128], id_f[0:64, 0:64])
        inc(eng.transpose(ps7[:, 192:256], vals[0:64, 128:256], id_f[0:64, 0:64]), "t")
    if _PH >= 2:
        onT(ttrans)
    tkt = CNT["t"]

    def tevac2(eng, tkt=tkt):
        W(eng, "t", tkt)
        eng.tensor_copy(out=idxT_i[:, 0:64], in_=ps7[:, 0:64])
        eng.tensor_copy(out=idxT_i[:, 64:128], in_=ps7[:, 64:128])
        eng.tensor_copy(out=ewT[:, 0:64], in_=ps7[:, 128:192])
        inc(eng.tensor_copy(out=ewT[:, 64:128], in_=ps7[:, 192:256]), "v")
    if _PH >= 2:
        onV(tevac2)
        psfree[6] = [("v", CNT["v"])]
    tkv = CNT["v"]

    IDXG_DONE = 0

    wb12 = [wb12_0, wb12_1]
    wb3 = [wb3_0, wb3_1]
    consumer_done = {}
    wload_cnt = {}

    def emit_wload(L):
        e, m = divmod(L, 3)
        buf = L % 2
        src = [w1_h, w2_h, w3_h][m]

        def wl(eng, e=e, m=m, buf=buf, L=L):
            if L == 0:
                W(eng, "t", PROJ_B1)
            elif L == 1:
                W(eng, "t", SIM_B1)
            else:
                W(eng, "t", consumer_done[L - 2])
            dst = wb12[buf] if m < 2 else wb3[buf]
            winc(eng.dma_start(out=dst[:, :], in_=src.ap()[e, :, :, :]), f"qw{L % 3}")
        onG(wl)
        wload_cnt[L] = (f"qw{L % 3}", CNT[f"qw{L % 3}"])

    eib = [ei0, ei1]
    gath_cnt = {}
    eiT_done = {}
    eiT_evac = {}
    eifm_free = {}

    def emit_gather(e):
        def gth(eng, e=e, prev=eiT_done.get(e - 1, 0)):
            if e == 0:
                W(eng, "t", SIM_B1)
                W(eng, "qlns", lnst[1])
                W(eng, "v", tkv)
            else:
                W(eng, "t", prev)
            for mt in range(4):
                bb, hh = divmod(mt, 2)
                be = bb * 32 + e
                winc(eng.indirect_dma_start(
                    out=ei_tm[:, mt, :],
                    out_offset=None,
                    in_=ln2f_d[:, :],
                    in_offset=bass.IndirectOffsetOnAxis(idxT_i[:, hh * 64 + be: hh * 64 + be + 1], 0)),
                    f"qg{e % 2}")
        onG(gth)
        gath_cnt[e] = (f"qg{e % 2}", CNT[f"qg{e % 2}"])

    def emit_eitrans(e):
        gq = gath_cnt[e]
        for mt in range(4):
            for dq in range(2):
                hb = dq % 2

                def eit(eng, e=e, mt=mt, dq=dq, hb=hb, gq=gq, fr=list(psfree[6 + hb])):
                    wfree(eng, [gq])
                    wfree(eng, fr)
                    if mt == 0 and dq == 0:
                        wfree(eng, eifm_free.get(e, []))
                    for c in range(4):
                        dc = dq * 4 + c
                        ins = eng.transpose(psb6[:, hb * 512 + c * 128: hb * 512 + (c + 1) * 128],
                                            ei_tm[:, mt, dc * 128:(dc + 1) * 128], id_bf[:, :])
                    inc(ins, "t")
                onT(eit)
                td = CNT["t"]

                def eie(eng, e=e, mt=mt, dq=dq, hb=hb, td=td):
                    W(eng, "t", td)
                    for c in range(4):
                        dc = dq * 4 + c
                        ins = eng.activation(out=eib[e % 2][:, dc * 512 + mt * 128: dc * 512 + mt * 128 + 128],
                                             in_=psb6[:, hb * 512 + c * 128: hb * 512 + (c + 1) * 128],
                                             func=AF.Copy)
                    inc(ins, "s")
                onS(eie)
                psfree[6 + hb] = [("s", CNT["s"])]
        eiT_done[e] = CNT["t"]
        eiT_evac[e] = ("s", CNT["s"])

    if _PH >= 3:
        emit_wload(0)
        emit_wload(1)
        emit_gather(0)
        emit_eitrans(0)
        emit_gather(1)
        emit_eitrans(1)

    sct_chain = {0: 0, 1: 0}
    dv_free = [[] for _ in range(4)]
    silfree = []
    down_prev = 0

    for e in (range(E) if _PH >= 3 else []):
        b12 = wb12[(3 * e) % 2]
        b22 = wb12[(3 * e + 1) % 2]
        b33 = wb3[(3 * e + 2) % 2]
        gateFc = {}
        valFc = {}
        silFc = {}

        for fc in range(FC):
            pg = fc % 2

            def gate(eng, e=e, fc=fc, pg=pg, b12=b12, fr=list(psfree[pg])):
                if fc == 0:
                    wfree(eng, [wload_cnt[3 * e]])
                    wfree(eng, [eiT_evac[e]])
                    if e == 0:
                        W(eng, "s", ATT_S)
                        W(eng, "v", ATT_V)
                wfree(eng, fr)
                for dc in range(DC):
                    ins = eng.matmul(ps[pg][:, :],
                                     lhsT=b12[:, dc * DFF + fc * 128: dc * DFF + fc * 128 + 128],
                                     rhs=eib[e % 2][:, dc * 512:(dc + 1) * 512],
                                     start=(dc == 0), stop=(dc == DC - 1))
                inc(ins, "t")
            onT(gate)
            gateFc[fc] = CNT["t"]

            def gsilu(eng, fc=fc, pg=pg, fr=list(silfree)):
                W(eng, "t", gateFc[fc])
                wfree(eng, fr)
                inc(eng.activation(out=sil_all[:, fc * 512:(fc + 1) * 512], in_=ps[pg][:, :], func=AF.Silu), "s")
            onS(gsilu)
            silFc[fc] = CNT["s"]
            psfree[pg] = [("s", CNT["s"])]
        consumer_done[3 * e] = gateFc[FC - 1]
        if 3 * e + 2 < 3 * E:
            emit_wload(3 * e + 2)

        for fc in range(FC):
            pv = 2 + fc % 2

            def value(eng, e=e, fc=fc, pv=pv, b22=b22, fr=list(psfree[pv])):
                if fc == 0:
                    wfree(eng, [wload_cnt[3 * e + 1]])
                wfree(eng, fr)
                for dc in range(DC):
                    ins = eng.matmul(ps[pv][:, :],
                                     lhsT=b22[:, dc * DFF + fc * 128: dc * DFF + fc * 128 + 128],
                                     rhs=eib[e % 2][:, dc * 512:(dc + 1) * 512],
                                     start=(dc == 0), stop=(dc == DC - 1))
                inc(ins, "t")
            onT(value)
            valFc[fc] = CNT["t"]

            def hmul(eng, e=e, fc=fc, pv=pv, dp=down_prev):
                W(eng, "t", valFc[fc])
                W(eng, "s", silFc[fc])
                if fc == 0 and e > 0:
                    W(eng, "t", dp)
                inc(eng.tensor_tensor(out=h_fm[:, fc * 512:(fc + 1) * 512],
                                      in0=ps[pv][:, :], in1=sil_all[:, fc * 512:(fc + 1) * 512], op=OP.mult), "v")
            onV(hmul)
            psfree[pv] = [("v", CNT["v"])]
        consumer_done[3 * e + 1] = valFc[FC - 1]
        HD = CNT["v"]
        silfree = [("v", HD)]
        eifm_free[e + 2] = [("t", valFc[FC - 1])]

        if 3 * e + 3 < 3 * E:
            emit_wload(3 * e + 3)
        if e + 2 < E:
            emit_gather(e + 2)
            emit_eitrans(e + 2)

        downT = {}
        dvdone = {}
        for mt in range(4):
            bb, hh = divmod(mt, 2)
            for dhf in range(2):
                pd = 4 + (mt * 2 + dhf) % 2

                def down(eng, e=e, mt=mt, dhf=dhf, pd=pd, HD=HD, b33=b33, fr=list(psfree[pd])):
                    if mt == 0 and dhf == 0:
                        wfree(eng, [wload_cnt[3 * e + 2]])
                        W(eng, "v", HD)
                    wfree(eng, fr)
                    for fc in range(FC):
                        ins = eng.matmul(ps[pd][:, :],
                                         lhsT=h_fm[:, fc * 512 + mt * 128: fc * 512 + mt * 128 + 128],
                                         rhs=b33[:, fc * D + dhf * 512: fc * D + dhf * 512 + 512],
                                         start=(fc == 0), stop=(fc == FC - 1))
                    inc(ins, "t")
                onT(down)
                downT[(mt, dhf)] = CNT["t"]

                def devac(eng, e=e, mt=mt, dhf=dhf, pd=pd, bb=bb, hh=hh, fr=list(dv_free[mt])):
                    W(eng, "t", downT[(mt, dhf)])
                    if dhf == 0:
                        wfree(eng, fr)
                    be = bb * 32 + e
                    inc(eng.tensor_scalar(dv[mt][:, dhf * 512:(dhf + 1) * 512], ps[pd][:, :],
                                          ewT[:, hh * 64 + be: hh * 64 + be + 1], None, op0=OP.mult), "v")
                onV(devac)
                psfree[pd] = [("v", CNT["v"])]
                dvdone[mt] = CNT["v"]
        consumer_done[3 * e + 2] = downT[(3, 1)]
        down_prev = downT[(3, 1)]
        if 3 * e + 4 < 3 * E:
            emit_wload(3 * e + 4)

        for bb in range(2):
            for hh in range(2):
                mt = bb * 2 + hh

                def sct(eng, e=e, mt=mt, bb=bb, hh=hh, dvd=dvdone[mt], chain=sct_chain[bb]):
                    W(eng, "v", dvd)
                    if e == 0:
                        W(eng, "qzf", ZFILL_DONE)
                    W(eng, f"qc{bb}", chain)
                    be = bb * 32 + e
                    winc(eng.indirect_dma_start(
                        out=accf_d[:, :],
                        out_offset=bass.IndirectOffsetOnAxis(idxT_i[:, hh * 64 + be: hh * 64 + be + 1], 0),
                        in_=dv[mt][:, :], in_offset=None,
                        compute_op=OP.add), f"qc{bb}")
                if _PH >= 4:
                    onG(sct)
                sct_chain[bb] = CNT[f"qc{bb}"]
            dv_free[bb * 2] = [(f"qc{bb}", CNT[f"qc{bb}"])]
            dv_free[bb * 2 + 1] = [(f"qc{bb}", CNT[f"qc{bb}"])]

    SCT_ALL = [("qc0", CNT["qc0"]), ("qc1", CNT["qc1"])]
    MOE_S = CNT["s"]
    MOE_V = CNT["v"]

    accfree = {}
    for b in range(B):
        for t in range(NT):
            par = t % 2
            sp = par * 16
            prev = (b, t - 2) if t >= 2 else ((b - 1, NT + t - 2) if b > 0 else None)

            def fld(eng, b=b, t=t, par=par, prev=prev):
                if b == 0 and t == 0:
                    wfree(eng, SCT_ALL)
                if prev is not None:
                    wfree(eng, accfree[prev])
                wfree(eng, [x1st[(b, t)]])
                winc(eng.dma_start(out=acc_t[par][:, :], in_=acc_d[b, t * 128:(t + 1) * 128, :]), f"qx{par}")
                winc(eng.dma_start(out=x1f_t[par][:, :], in_=x1_d[b, t * 128:(t + 1) * 128, :]), f"qx{par}")
            onQ(fld)
            fq = CNT[f"qx{par}"]

            def fstat(eng, par=par, sp=sp, fq=fq, b=b, t=t):
                W(eng, f"qx{par}", fq)
                if b == 0 and t == 0:
                    W(eng, "s", MOE_S)
                    W(eng, "v", MOE_V)
                inc(ln_stats_v(eng, acc_t[par][:, :], sp, scrf[:, :]), "v")
            onV(fstat)
            vst = CNT["v"]

            def fsq(eng, sp=sp, vst=vst):
                W(eng, "v", vst)
                inc(eng.activation(out=stc[:, sp + 6:sp + 7], in_=stc[:, sp + 5:sp + 6],
                                   func=AF.Sqrt, bias=EPS), "s")
            onS(fsq)
            sst = CNT["s"]

            def frc(eng, sp=sp, sst=sst, par=par, fr=list(outtfree[par])):
                W(eng, "s", sst)
                wfree(eng, fr)
                vnext(eng, ln_rstd_v(eng, sp))
                vnext(eng, ln_apply_v(eng, out_t[par][:, :], acc_t[par][:, :], sp))
                inc(eng.tensor_add(out_t[par][:, :], out_t[par][:, :], x1f_t[par][:, :]), "v")
            onV(frc)
            accfree[(b, t)] = [("v", CNT["v"])]
            vfin = CNT["v"]

            def fst(eng, b=b, t=t, par=par, vfin=vfin):
                W(eng, "v", vfin)
                winc(eng.dma_start(out=y_d[b, t * 128:(t + 1) * 128, :], in_=out_t[par][:, :]), f"qf{par}")
            onQ(fst)
            outtfree[par] = [(f"qf{par}", CNT[f"qf{par}"])]

    FIN = [("qf0", CNT["qf0"]), ("qf1", CNT["qf1"])]

    def fin(eng, FIN=FIN):
        wfree(eng, FIN)
    onQ(fin)

    block_ctx.__exit__(None, None, None)
    for c in reversed(list(sem_ctx.values())):
        c.__exit__(None, None, None)
    for c in reversed(ps_ctx):
        c.__exit__(None, None, None)
    arena_ctx.__exit__(None, None, None)
    return nc


def kernel(L0_attn_x, L0_attn_W_qk, L0_attn_b_qk, L0_ffn_router_w1, L0_ffn_router_w2,
           L0_ffn_w1, L0_ffn_w2, L0_ffn_w3):
    x = np.ascontiguousarray(L0_attn_x, dtype=np.float32)
    Wqk = np.asarray(L0_attn_W_qk, dtype=np.float32)
    bqk_in = np.asarray(L0_attn_b_qk, dtype=np.float32)
    Wcat = np.concatenate([Wqk[0::2], Wqk[1::2]], axis=0)
    wqkT = np.ascontiguousarray(Wcat.T.reshape(DC, 128, 2048).transpose(1, 0, 2))
    bq = bqk_in[0::2].reshape(8, 128).T
    bk = bqk_in[1::2].reshape(8, 128).T
    bqk = np.ascontiguousarray(np.concatenate([bq, bk], axis=1))
    rw1T = np.ascontiguousarray(np.asarray(L0_ffn_router_w1, dtype=np.float32)
                                .reshape(DC, 128, RH).transpose(1, 0, 2))
    rw2 = np.ascontiguousarray(L0_ffn_router_w2, dtype=np.float32)
    w1h = np.ascontiguousarray(np.asarray(L0_ffn_w1, dtype=np.float32)
                               .reshape(E, DC, 128, DFF).transpose(0, 2, 1, 3))
    w2h = np.ascontiguousarray(np.asarray(L0_ffn_w2, dtype=np.float32)
                               .reshape(E, DC, 128, DFF).transpose(0, 2, 1, 3))
    w3h = np.ascontiguousarray(np.asarray(L0_ffn_w3, dtype=np.float32)
                               .reshape(E, FC, 128, D).transpose(0, 2, 1, 3))
    idbf = np.eye(128, dtype=ml_dtypes.bfloat16)
    idf32 = np.eye(128, dtype=np.float32)

    nc = build_nc()
    in_maps = []
    for c in range(8):
        in_maps.append({
            "x": np.ascontiguousarray(x[2 * c:2 * c + 2]),
            "wqkT": wqkT, "bqk": bqk, "rw1T": rw1T, "rw2": rw2,
            "w1h": w1h, "w2h": w2h, "w3h": w3h,
            "idbf": idbf, "idf32": idf32,
        })
    res = run_bass_kernel_spmd(nc, in_maps[:_NCORES], core_ids=list(range(_NCORES)))
    out = np.concatenate([np.asarray(r["y"]) for r in res.results], axis=0)
    if _NCORES < 8:
        out = np.concatenate([out] + [out[0:2]] * (8 - _NCORES), axis=0)
    return np.ascontiguousarray(out.reshape(16, S, D).astype(np.float32))


if __name__ == "__main__":
    nc = build_nc()
    print("build ok; instructions:", sum(1 for _ in nc.all_instructions()))
